# version 34
# baseline (speedup 1.0000x reference)
import math
import numpy as np

import concourse.bass as bass
import concourse.mybir as mybir
from concourse.bass_utils import run_bass_kernel_spmd

F32 = mybir.dt.float32
BF16 = mybir.dt.bfloat16
Alu = mybir.AluOpType
Act = mybir.ActivationFunctionType
AX = mybir.AxisListType.X

D = 1024
CH = 128
TQ = 512
NTQ = TQ // CH


class Step:
    __slots__ = ("i", "engine", "fn", "deps", "is_dma", "chan", "inc", "target",
                 "target_pfx", "chan_before", "tag")

    def __init__(self, i, engine, fn, deps, is_dma, chan, tag):
        self.i, self.engine, self.fn = i, engine, fn
        self.deps, self.is_dma, self.tag = deps, is_dma, tag
        self.chan = chan
        self.inc = False
        self.target = 0
        self.target_pfx = 0
        self.chan_before = 0


class Buf:
    __slots__ = ("name", "writers", "readers")

    def __init__(self, name):
        self.name = name
        self.writers = []
        self.readers = []


class Sched:
    def __init__(self):
        self.steps = []

    def add(self, engine, fn, reads=(), writes=(), deps=(), is_dma=False, chan=None, tag=""):
        dep = []
        seen = set()

        def _push(d):
            if d is not None and id(d) not in seen:
                seen.add(id(d))
                dep.append(d)

        for b in reads:
            for d in b.writers:
                _push(d)
        for b in writes:
            for d in b.writers:
                _push(d)
            for d in b.readers:
                _push(d)
        for d in deps:
            _push(d)
        st = Step(len(self.steps), engine, fn, dep, is_dma, chan, tag)
        for b in reads:
            b.readers.append(st)
        for b in writes:
            if b.readers:
                b.writers = [st]
                b.readers = []
            else:
                b.writers.append(st)
        st.deps = [d for d in st.deps if d is not st]
        self.steps.append(st)
        return st

    BURST = ("gsetup",)

    def emit(self, nc, blk, dma_weights=None, count_mode=False):
        for st in self.steps:
            if st.is_dma:
                st.inc = True
                assert st.chan, "dma step %s needs a channel" % st.tag
            for d in st.deps:
                if d.engine != st.engine or d.is_dma:
                    d.inc = True
        def skey(st):
            return ("D", st.chan) if st.is_dma else (st.engine, False)
        cnt = {}
        for st in self.steps:
            key = skey(st)
            if st.inc:
                w = 1
                if st.is_dma:
                    w = 16 * (dma_weights.get(st.i, 1) if dma_weights else 1)
                st.chan_before = cnt.get(key, 0)
                cnt[key] = cnt.get(key, 0) + w
                st.target = st.target_pfx = cnt[key]
        for st in self.steps:
            if st.is_dma and st.chan in self.BURST:
                st.target = cnt[("D", st.chan)]
        sems = {}
        for key in cnt:
            sems[key] = nc.alloc_semaphore("sem_%s_%s" % (key[0], key[1]))
        mysteps = {}
        for st in self.steps:
            mysteps.setdefault(st.engine, []).append(st)

        def section(engine_name):
            def run(eng):
                waited = {}
                local_idx = {}
                drained_upto = -1
                nloc = 0
                for st in mysteps.get(engine_name, ()):
                    need_drain = False
                    waits = {}
                    for d in st.deps:
                        if d.engine == engine_name and not d.is_dma:
                            if local_idx[id(d)] > drained_upto:
                                need_drain = True
                        elif st.is_dma and d.is_dma and d.chan == st.chan:
                            key = ("D", d.chan)
                            if st.chan_before > waited.get(key, 0):
                                waits[key] = max(waits.get(key, 0), st.chan_before)
                        else:
                            key = ("D", d.chan) if d.is_dma else (d.engine, False)
                            if d.target > waited.get(key, 0):
                                waits[key] = max(waits.get(key, 0), d.target)
                    if need_drain and engine_name in ("vector", "scalar", "gpsimd"):
                        eng.drain()
                        drained_upto = nloc - 1
                    for key, tgt in waits.items():
                        eng.wait_ge(sems[key], tgt)
                        waited[key] = tgt
                    inst = st.fn(eng)
                    if st.inc:
                        assert inst is not None, "step %s needs inc but returned None" % st.tag
                        inst.then_inc(sems[skey(st)], 16 if st.is_dma else 1)
                    if count_mode and st.is_dma:
                        eng.nop()
                    local_idx[id(st)] = nloc
                    nloc += 1
            return run

        blk.tensor(section("tensor"))
        blk.vector(section("vector"))
        blk.scalar(section("scalar"))
        blk.gpsimd(section("gpsimd"))
        blk.sync(section("sync"))


def make_consts():
    n = np.arange(256, dtype=np.float64)
    jj = np.arange(128, dtype=np.float64)
    F = np.zeros((2, 256, 128), np.float64)
    F[0] = np.cos(2 * np.pi * np.outer(n, jj) / 256.0)
    F[1, :, 0] = np.cos(np.pi * n)
    F[1, :, 1:] = -np.sin(2 * np.pi * np.outer(n, np.arange(1, 128, dtype=np.float64)) / 256.0)
    G = np.zeros((256, 128), np.float64)
    m = 128.0 + jj
    G[0, :] = 1.0 / 256.0
    p0 = np.arange(1, 128, dtype=np.float64)
    G[1:128, :] = (2.0 / 256.0) * np.cos(2 * np.pi * np.outer(p0, m) / 256.0)
    G[128, :] = ((-1.0) ** np.arange(128)) / 256.0
    G[129:, :] = (-2.0 / 256.0) * np.sin(2 * np.pi * np.outer(p0, m) / 256.0)

    def hi(a):
        return np.ascontiguousarray(a.astype(np.float32))

    def lo(a):
        return np.ascontiguousarray((a - a.astype(np.float32).astype(np.float64)).astype(np.float32))

    return {
        "cFhi": hi(F), "cFlo": lo(F),
        "cGhi": hi(G), "cGlo": lo(G),
        "cI": np.eye(128, dtype=np.float32),
        "cSel": np.repeat(np.eye(8, dtype=np.float32), 64, axis=1),
    }


WEIGHTS = [
    "L0_attn_W_in", "L0_attn_b_in", "L0_attn_k0", "L0_attn_k1",
    "L0_attn_gamma", "L0_attn_beta", "L0_attn_W_out", "L0_attn_b_out",
    "L0_ffn_enc_W", "L0_ffn_enc_b", "L0_ffn_s2c_W", "L0_ffn_s2c_b",
    "L0_ffn_rW1", "L0_ffn_rb1", "L0_ffn_rW2", "L0_ffn_rb2",
    "L0_ffn_We1", "L0_ffn_be1", "L0_ffn_We2", "L0_ffn_be2",
    "L0_ffn_c2s_W", "L0_ffn_c2s_b", "L0_ffn_dec_W", "L0_ffn_dec_b",
    "L0_ffn_ln_g", "L0_ffn_ln_b",
]


def _build(T, until="full"):
    NQ = T // TQ
    NCHK = T // CH
    nc = bass.Bass("TRN2", target_bir_lowering=False)

    dx = nc.dram_tensor("x", [T, D], F32, kind="ExternalInput").ap()
    dy = nc.dram_tensor("y", [T, D], F32, kind="ExternalOutput").ap()
    _wshapes = [
              ("L0_attn_W_in", [D, 3 * D]), ("L0_attn_b_in", [3 * D]),
              ("L0_attn_k0", [1, D, 128]), ("L0_attn_k1", [1, D, 128]),
              ("L0_attn_gamma", [D]), ("L0_attn_beta", [D]),
              ("L0_attn_W_out", [D, D]), ("L0_attn_b_out", [D]),
              ("L0_ffn_enc_W", [D, 2048]), ("L0_ffn_enc_b", [2048]),
              ("L0_ffn_s2c_W", [2048, 64]), ("L0_ffn_s2c_b", [64]),
              ("L0_ffn_rW1", [64, 64]), ("L0_ffn_rb1", [64]),
              ("L0_ffn_rW2", [64, 8]), ("L0_ffn_rb2", [8]),
              ("L0_ffn_We1", [8, 64, D]), ("L0_ffn_be1", [8, D]),
              ("L0_ffn_We2", [8, D, 64]), ("L0_ffn_be2", [8, 64]),
              ("L0_ffn_c2s_W", [64, 2048]), ("L0_ffn_c2s_b", [2048]),
              ("L0_ffn_dec_W", [2048, D]), ("L0_ffn_dec_b", [D]),
              ("L0_ffn_ln_g", [D]), ("L0_ffn_ln_b", [D]),
          ]
    dw = {n_: nc.dram_tensor(n_, s, F32, kind="ExternalInput").ap()
          for n_, s in _wshapes}
    dcFhi = nc.dram_tensor("cFhi", [2, 256, 128], F32, kind="ExternalInput").ap()
    dcFlo = nc.dram_tensor("cFlo", [2, 256, 128], F32, kind="ExternalInput").ap()
    dcGhi = nc.dram_tensor("cGhi", [256, 128], F32, kind="ExternalInput").ap()
    dcGlo = nc.dram_tensor("cGlo", [256, 128], F32, kind="ExternalInput").ap()
    dcI = nc.dram_tensor("cI", [128, 128], F32, kind="ExternalInput").ap()
    dcSel = nc.dram_tensor("cSel", [8, 512], F32, kind="ExternalInput").ap()
    dwe1h = nc.dram_tensor("we1hd", [8, 128, 1024], BF16, kind="Internal").ap()
    dwe1l = nc.dram_tensor("we1ld", [8, 128, 1024], BF16, kind="Internal").ap()
    dwe2h = nc.dram_tensor("we2hd", [8, 1024, 64], BF16, kind="Internal").ap()
    dwe2l = nc.dram_tensor("we2ld", [8, 1024, 64], BF16, kind="Internal").ap()

    def sb(name, p, cols, dt=F32):
        return nc.alloc_sbuf_tensor(name, [p, cols], dt).ap()

    s_x = sb("s_x", 128, 4096)
    s_z = sb("s_z", 128, 4096)
    s_xn = sb("s_xn", 128, 4096)
    s_at = sb("s_at", 128, 4096)
    s_v = sb("s_v", 128, 4096)
    s_big = sb("s_big", 128, 8192)
    s_wb = sb("s_wb", 128, 2048)
    s_cry = sb("s_cry", 128, 2048)
    s_ksp = sb("s_ksp", 128, 4096)
    s_cons = sb("s_cons", 128, 13 * 128)
    s_gb = sb("s_gb", 128, 2048)
    s_bias = sb("s_bias", 128, 256)
    s_wtmp = sb("s_wtmp", 128, 2560)
    s_st = sb("s_st", 128, 164)
    s_gtm = sb("s_gtm", 128, 32)
    s_gw = sb("s_gw", 128, 64)
    s_cont = sb("s_cont", 64, 512)
    s_of = sb("s_of", 64, 512)
    s_eo = sb("s_eo", 64, 512)
    s_lg = sb("s_lg", 8, 512)
    s_s2c = sb("s_s2c", 128, 1024)
    s_rw1 = sb("s_rw1", 64, 64)
    s_rw2 = sb("s_rw2", 64, 8)
    s_encb = sb("s_encb", 128, 16)
    s_c2sb = sb("s_c2sb", 128, 16)
    s_decb = sb("s_decb", 128, 8)
    s_be1 = sb("s_be1", 128, 64)
    s_be2 = sb("s_be2", 64, 8)
    s_s2cb = sb("s_s2cb", 64, 1)
    s_rb1 = sb("s_rb1", 64, 1)
    s_rb2 = sb("s_rb2", 8, 1)
    s_gA = sb("s_gA", 128, 8)
    s_bA = sb("s_bA", 128, 8)
    s_ones = sb("s_ones", 1, 64)
    s_sel = sb("s_sel", 8, 512)
    s_we1h = sb("s_we1h", 128, 2048, BF16)
    s_we1l = sb("s_we1l", 128, 2048, BF16)
    s_we2h = sb("s_we2h", 128, 1024, BF16)
    s_we2l = sb("s_we2l", 128, 1024, BF16)
    s_hh = sb("s_hh", 128, 2048, BF16)
    s_hl = sb("s_hl", 128, 2048, BF16)
    s_conth = sb("s_conth", 128, 512, BF16)
    s_contl = sb("s_contl", 128, 512, BF16)

    pb = [nc.alloc_psum_tensor("pb%d" % i, [128, 512], F32).ap() for i in range(8)]

    def cons(slot):
        return s_cons[:, slot * 128:(slot + 1) * 128]
    FWD_HI = lambda s, kc: cons(s * 2 + kc)
    FWD_LO = lambda s, kc: cons(4 + s * 2 + kc)
    G_HI = lambda kc: cons(8 + kc)
    G_LO = lambda kc: cons(10 + kc)
    IDENT = cons(12)

    S = Sched()

    Bx = [Buf("x%d" % t) for t in range(NTQ)]
    Bz = [Buf("z%d" % t) for t in range(NTQ)]
    Bxn = [Buf("xn%d" % t) for t in range(NTQ)]
    Bat = [[Buf("at%d_%d" % (k, t)) for t in range(NTQ)] for k in range(8)]
    Bv = [Buf("v%d" % t) for t in range(NTQ)]
    Bbig = [Buf("big%d" % g) for g in range(16)]
    Bwb = [Buf("wb%d" % i) for i in range(2)]
    Bwe1 = [Buf("we1_%d" % i) for i in range(2)]
    Bwe2s = [Buf("we2s_%d" % i) for i in range(2)]
    Bbias = [Buf("bias%d" % i) for i in range(2)]
    Bp = [Buf("p%d" % i) for i in range(8)]
    Bcry_v = Buf("cry_v")
    Bcry_y = Buf("cry_y")
    Bst = [Buf("st%d" % i) for i in range(4)]
    Bgst = [Buf("gst%d" % i) for i in range(4)]
    Bgtm = Buf("gtm")
    Bgw = Buf("gw")
    Bcont = Buf("cont")
    Bcsp = Buf("csp")
    Bt1 = Buf("t1")
    Bof = Buf("of")
    Beo = Buf("eo")
    Bgbc = Buf("gbc")
    Blg = Buf("lg")
    Bwtmp = Buf("wtmp")
    Bwt = [Buf("wt0"), Buf("wt1")]
    Bhh = [Buf("hh%d" % i) for i in range(4)]
    Bhl = [Buf("hl%d" % i) for i in range(4)]
    Bksp = Buf("ksp")
    Bsetup = Buf("setup")
    Bgb = Buf("gb")

    def dma(engine, dst, src, reads=(), writes=(), deps=(), chan="gsetup", tag=""):
        slow = engine == "gpsimd"
        return S.add(engine,
                     lambda e, d_=dst, s_=src: e.dma_start(
                         d_, s_, allow_slow_non_contiguous=slow),
                     reads=reads, writes=writes, deps=deps, is_dma=True, chan=chan,
                     tag=tag)

    def xtile(buf_ap, t):
        return buf_ap[:, t * D:(t + 1) * D]

    def _memset_fn(v):
        v.memset(s_ones[:, :], 1.0)
        v.memset(s_st[:, 128:129], 1e-5)
        v.memset(s_cry[:, :], 0.0)
        v.memset(s_of[:, :], 0.0)
        v.memset(s_st[:, 129:157], 0.0)
        v.memset(s_st[:, 124:125], 1e-10)
        v.memset(s_st[:, 126:127], -1.0)
        return v.memset(s_st[:, 125:126], 1.0 + 1e-5)
    st_memset = S.add("vector", _memset_fn,
                      writes=[Bsetup, Bcry_v, Bcry_y, Bof], tag="memset")
    EPS = s_st[:, 128:129]
    EPS2 = s_st[:, 124:125]
    CM1 = s_st[:, 126:127]
    Z0 = s_st[:, 129:130]
    ONEPE = s_st[:, 125:126]

    dma("gpsimd", s_cons[:, 0:512],
        dcFhi.rearrange("s (kc p) b -> p s kc b", p=128), writes=[Bsetup], tag="cFhi")
    dma("gpsimd", s_cons[:, 512:1024],
        dcFlo.rearrange("s (kc p) b -> p s kc b", p=128), writes=[Bsetup], tag="cFlo")
    dma("gpsimd", s_cons[:, 1024:1280],
        dcGhi.rearrange("(kc p) b -> p kc b", p=128), writes=[Bsetup], tag="cGhi")
    dma("gpsimd", s_cons[:, 1280:1536],
        dcGlo.rearrange("(kc p) b -> p kc b", p=128), writes=[Bsetup], tag="cGlo")
    dma("gpsimd", s_cons[:, 1536:1664], dcI[:, :], writes=[Bsetup], tag="cI")
    dma("gpsimd", s_sel[:, :], dcSel[:, :], writes=[Bsetup], tag="cSel")
    dma("gpsimd", s_gb[:, 0:1024],
        dw["L0_ffn_ln_g"].unsqueeze(0).broadcast_to([128, 1024]), writes=[Bgb], tag="lng")
    dma("gpsimd", s_gb[:, 1024:2048],
        dw["L0_ffn_ln_b"].unsqueeze(0).broadcast_to([128, 1024]), writes=[Bgb], tag="lnb")
    dma("gpsimd", s_gA[:, :], dw["L0_attn_gamma"].rearrange("(kc p) -> p kc", p=128),
        writes=[Bsetup], tag="gA")
    dma("gpsimd", s_bA[:, :], dw["L0_attn_beta"].rearrange("(kc p) -> p kc", p=128),
        writes=[Bsetup], tag="bA")
    dma("gpsimd", s_s2c[:, :], dw["L0_ffn_s2c_W"].rearrange("(kc p) m -> p kc m", p=128),
        writes=[Bsetup], tag="s2cW")
    dma("gpsimd", s_rw1[:, :], dw["L0_ffn_rW1"][:, :], writes=[Bsetup], tag="rW1")
    dma("gpsimd", s_rw2[:, :], dw["L0_ffn_rW2"][:, :], writes=[Bsetup], tag="rW2")
    dma("gpsimd", s_encb[:, :], dw["L0_ffn_enc_b"].rearrange("(mc p) -> p mc", p=128),
        writes=[Bsetup], tag="encb")
    dma("gpsimd", s_c2sb[:, :], dw["L0_ffn_c2s_b"].rearrange("(mc p) -> p mc", p=128),
        writes=[Bsetup], tag="c2sb")
    dma("gpsimd", s_decb[:, :], dw["L0_ffn_dec_b"].rearrange("(mc p) -> p mc", p=128),
        writes=[Bsetup], tag="decb")
    dma("gpsimd", s_be1[:, :], dw["L0_ffn_be1"].rearrange("e (mc p) -> p e mc", p=128),
        writes=[Bsetup], tag="be1")
    dma("gpsimd", s_be2[:, :], dw["L0_ffn_be2"].rearrange("e p -> p e"),
        writes=[Bsetup], tag="be2")
    dma("gpsimd", s_s2cb[:, :], dw["L0_ffn_s2c_b"].unsqueeze(1), writes=[Bsetup], tag="s2cb")
    dma("gpsimd", s_rb1[:, :], dw["L0_ffn_rb1"].unsqueeze(1), writes=[Bsetup], tag="rb1")
    dma("gpsimd", s_rb2[:, :], dw["L0_ffn_rb2"].unsqueeze(1), writes=[Bsetup], tag="rb2")

    for cj, kname in enumerate(("L0_attn_k0", "L0_attn_k1")):
        dma("gpsimd", s_wtmp[:, 0:1024],
            dw[kname][0].rearrange("(c p) t -> p c t", p=128),
            writes=[Bwt[0]], deps=(), chan="ktap", tag="kload%d" % cj)
        for c4 in range(2):
            def tp_fn(t, c4=c4):
                last = None
                for ci in range(4):
                    c = c4 * 4 + ci
                    last = t.transpose(pb[7][:, ci * 128:(ci + 1) * 128],
                                       s_wtmp[:, c * 128:(c + 1) * 128], IDENT)
                return last
            stp = S.add("tensor", tp_fn, reads=[Bwt[0], Bsetup], writes=[Bp[7]],
                        tag="ktr%d_%d" % (cj, c4))
            S.add("vector", lambda v, c4=c4: v.tensor_copy(
                s_wtmp[:, 1024 + c4 * 512:1024 + (c4 + 1) * 512], pb[7][:, :]),
                reads=[Bp[7]], writes=[Bwt[1]], tag="ktr_ev%d_%d" % (cj, c4))
        for s_i in range(2):
            for half in range(2):
                def kf_fn(t, s_i=s_i, half=half):
                    rhs = s_wtmp[:, 1024 + half * 512:1024 + (half + 1) * 512]
                    t.matmul(pb[s_i][:, :], FWD_HI(s_i, 0), rhs, start=True, stop=False)
                    return t.matmul(pb[s_i][:, :], FWD_LO(s_i, 0), rhs, start=False, stop=True)
                S.add("tensor", kf_fn, reads=[Bwt[1], Bsetup], writes=[Bp[s_i]],
                      tag="kfwd%d_%d_%d" % (cj, s_i, half))
                S.add("vector", lambda v, s_i=s_i, half=half, cj=cj: v.tensor_copy(
                    s_ksp[:, cj * 2048 + s_i * 1024 + half * 512:][:, 0:512],
                    pb[s_i][:, :]),
                    reads=[Bp[s_i]], writes=[Bksp], tag="kev%d_%d_%d" % (cj, s_i, half))

    def KA(cj, s_i):
        return s_ksp[:, cj * 2048 + s_i * 1024: cj * 2048 + (s_i + 1) * 1024]

    Bpw1 = Buf("pw_we1")
    Bpw2 = Buf("pw_we2")

    def emit_presplit():
        ps_outs = []
        for e in range(8):
            slot = e % 2
            Bw1s = [Bbig[8 + 2 * slot], Bbig[9 + 2 * slot]]
            w32 = s_big[0:64, 4096 + slot * 1024:4096 + (slot + 1) * 1024]
            hst = s_we1h[0:64, 0:1024]
            lst = s_we1l[0:64, 0:1024]
            dma("gpsimd", w32, dw["L0_ffn_We1"][e, :, :], writes=Bw1s,
                chan="pi%d" % slot, tag="pw1i%d" % e)
            S.add("scalar", lambda sc, hst=hst, w32=w32: sc.copy(hst, w32),
                  reads=Bw1s, writes=[*Bw1s, Bwe1[0]], tag="pw1c%d" % e)
            S.add("gpsimd", lambda g, lst=lst, w32=w32, hst=hst: g.tensor_tensor(
                lst, w32, hst, Alu.subtract),
                reads=[*Bw1s, Bwe1[0]], writes=[*Bw1s, Bwe1[0]],
                tag="pw1s%d" % e)
            for half in range(2):
                ps_outs.append(dma(
                    "gpsimd", dwe1h[e, half * 64:(half + 1) * 64, :], hst,
                    reads=[Bwe1[0]],
                    chan="ph%d" % slot, tag="pw1oh%d_%d" % (e, half)))
                ps_outs.append(dma(
                    "gpsimd", dwe1l[e, half * 64:(half + 1) * 64, :], lst,
                    reads=[Bwe1[0]],
                    chan="pl%d" % slot, tag="pw1ol%d_%d" % (e, half)))
            Bw2s = [Bbig[12 + slot]]
            w232 = s_big[:, 6144 + slot * 512:6144 + (slot + 1) * 512]
            dma("gpsimd", w232.rearrange("p (hc m) -> p hc m", m=64),
                dw["L0_ffn_We2"][e, :, :].rearrange("(hc p) m -> p hc m", p=128),
                writes=Bw2s, chan="pi%d" % slot, tag="pw2i%d" % e)
            S.add("scalar", lambda sc, w232=w232: sc.copy(s_we2h[:, 0:512], w232),
                  reads=Bw2s, writes=[*Bw2s, Bwe2s[0]], tag="pw2c%d" % e)
            S.add("gpsimd", lambda g, w232=w232: g.tensor_tensor(
                s_we2l[:, 0:512], w232, s_we2h[:, 0:512], Alu.subtract),
                reads=[*Bw2s, Bwe2s[0]], writes=[*Bw2s, Bwe2s[0]], tag="pw2s%d" % e)
            ps_outs.append(dma(
                "gpsimd", dwe2h[e, :, :].rearrange("(hc p) m -> p hc m", p=128),
                s_we2h[:, 0:512].rearrange("p (hc m) -> p hc m", m=64),
                reads=[Bwe2s[0]], chan="qh%d" % slot, tag="pw2oh%d" % e))
            ps_outs.append(dma(
                "gpsimd", dwe2l[e, :, :].rearrange("(hc p) m -> p hc m", p=128),
                s_we2l[:, 0:512].rearrange("p (hc m) -> p hc m", m=64),
                reads=[Bwe2s[0]], chan="ql%d" % slot, tag="pw2ol%d" % e))
        S.add("gpsimd", lambda e_: e_.nop(), deps=ps_outs, writes=[Bpw1, Bpw2],
              tag="ps_fence")


    def ln_steps(src_ap, dst_ap, rb, wb_, tag, slot=0, extra_reads=(), stage=None):
        o = slot * 16
        Bs = Bst[slot]
        if stage not in (None, "sa"):
            st1 = None
        else:
            st1 = S.add("vector", lambda v: (
            v.bn_stats(s_st[:, o:o + 6], src_ap[:, 0:512]),
            v.bn_stats(s_st[:, o + 6:o + 12], src_ap[:, 512:1024]))[-1],
            reads=[*rb, *extra_reads], writes=[Bs], tag=tag + "_stats")
            S.add("vector", lambda v: v.bn_aggr(s_st[:, o + 12:o + 14], s_st[:, o:o + 12]),
                  reads=[Bs], writes=[Bs], tag=tag + "_aggr")
        if stage in (None, "sq"):
            S.add("scalar", lambda sc: (
            sc.activation(s_st[:, o + 14:o + 15], s_st[:, o + 13:o + 14],
                          Act.Sqrt, bias=EPS),
            sc.drain(),
            sc.activation(s_st[:, o + 9:o + 10], s_st[:, o + 12:o + 13],
                          Act.Identity, bias=Z0, scale=CM1))[-1],
            reads=[Bs, Bsetup], writes=[Bs], tag=tag + "_sqrt")
        if stage in (None, "rc"):
            S.add("vector", lambda v: v.reciprocal(s_st[:, o + 15:o + 16], s_st[:, o + 14:o + 15]),
                  reads=[Bs], writes=[Bs], tag=tag + "_recip")
        if stage not in (None, "ap"):
            return None
        st5 = S.add("scalar", lambda sc: (
            sc.activation(s_st[:, o + 8:o + 9], s_st[:, o + 15:o + 16],
                          Act.Identity, bias=Z0, scale=s_st[:, o + 9:o + 10]),
            sc.drain(),
            sc.activation(dst_ap, src_ap, Act.Identity,
                          bias=s_st[:, o + 8:o + 9], scale=s_st[:, o + 15:o + 16]))[-1],
            reads=[Bs, Bsetup, *rb], writes=[*wb_, Bs], tag=tag + "_apply")
        return st5

    Bst2 = [Buf("st2_%d" % i) for i in range(4)]

    O12 = [88, 104, 130, 146]

    def emit_ln12(qq, t, src, src_reads, stage=None):
        o = O12[t]
        Bs = Bst2[t]
        if stage in (None, "sa"):
            S.add("vector", lambda v, o=o, src=src: (
            v.bn_stats(s_st[:, o:o + 6], src[:, 0:512]),
            v.bn_stats(s_st[:, o + 6:o + 12], src[:, 512:1024]))[-1],
                reads=list(src_reads), writes=[Bs], tag="q%d_ln12_%d_stats" % (qq, t))
            S.add("vector", lambda v, o=o: v.bn_aggr(
                s_st[:, o + 12:o + 14], s_st[:, o:o + 12]),
                reads=[Bs], writes=[Bs], tag="q%d_ln12_%d_aggr" % (qq, t))
        if stage in (None, "sq"):
            S.add("scalar", lambda sc, o=o: sc.activation(
                s_st[:, o + 8:o + 9], s_st[:, o + 13:o + 14], Act.Sqrt, bias=EPS),
                reads=[Bs, Bsetup], writes=[Bs], tag="q%d_ln12_%d_s1" % (qq, t))
            S.add("scalar", lambda sc, o=o: (
            sc.activation(s_st[:, o:o + 1], s_st[:, o + 13:o + 14],
                          Act.Identity, bias=EPS2, scale=EPS),
            sc.drain(),
            sc.activation(s_st[:, o + 9:o + 10], s_st[:, o + 13:o + 14],
                          Act.Sqrt, bias=s_st[:, o:o + 1]))[-1],
                reads=[Bs, Bsetup], writes=[Bs], tag="q%d_ln12_%d_s2" % (qq, t))
            S.add("scalar", lambda sc, o=o: sc.activation(
                s_st[:, o + 10:o + 11], s_st[:, o + 12:o + 13],
                Act.Identity, bias=Z0, scale=CM1),
                reads=[Bs, Bsetup], writes=[Bs], tag="q%d_ln12_%d_nm" % (qq, t))
        if stage in (None, "rc"):
            S.add("vector", lambda v, o=o: v.reciprocal(
                s_st[:, o + 14:o + 16], s_st[:, o + 8:o + 10]),
                reads=[Bs], writes=[Bs], tag="q%d_ln12_%d_rec" % (qq, t))
        if stage not in (None, "ap"):
            return
        S.add("scalar", lambda sc, o=o, src=src, t=t: (
            sc.activation(s_st[:, o + 11:o + 12], s_st[:, o + 14:o + 15],
                          Act.Identity, bias=Z0, scale=s_st[:, o + 10:o + 11]),
            sc.drain(),
            sc.activation(xtile(s_z, t), src, Act.Identity,
                          bias=s_st[:, o + 11:o + 12], scale=s_st[:, o + 14:o + 15]))[-1],
            reads=[Bs, Bsetup, *src_reads], writes=[Bz[t], Bs],
            tag="q%d_ln12_%d_zap" % (qq, t))
        S.add("scalar", lambda sc, o=o, src=src, t=t: (
            sc.activation(s_st[:, o + 11:o + 12], s_st[:, o + 15:o + 16],
                          Act.Identity, bias=Z0, scale=s_st[:, o + 10:o + 11]),
            sc.drain(),
            sc.activation(xtile(s_xn, t), src, Act.Identity,
                          bias=s_st[:, o + 11:o + 12], scale=s_st[:, o + 15:o + 16]))[-1],
            reads=[Bs, Bsetup, *src_reads], writes=[Bxn[t], Bs],
            tag="q%d_ln12_%d_xap" % (qq, t))

    def transpose_tile_to_at(tt, src_buf_ap, rbufs, tag, scale_col=None, bias_col=None):
        if True:
            for kc in range(8):
                bank = 4 + kc % 4
                S.add("tensor", lambda t, kc=kc, bank=bank, tt=tt: t.transpose(
                    pb[bank][:, tt * 128:(tt + 1) * 128],
                    src_buf_ap[:, tt * D + kc * 128: tt * D + (kc + 1) * 128],
                    IDENT),
                    reads=[*rbufs, Bsetup], writes=[Bp[bank]],
                    tag=tag + "_tp%d_%d" % (kc, tt))
                dst = s_at[:, kc * 512 + tt * 128: kc * 512 + (tt + 1) * 128]
                psl = pb[bank][:, tt * 128:(tt + 1) * 128]
                if scale_col is not None:
                    S.add("scalar", lambda sc, kc=kc, dst=dst, psl=psl: sc.activation(
                        dst, psl, Act.Identity,
                        bias=bias_col[:, kc:kc + 1], scale=scale_col[:, kc:kc + 1]),
                        reads=[Bp[bank], Bsetup], writes=[Bat[kc][tt]],
                        tag=tag + "_ev%d_%d" % (kc, tt))
                else:
                    S.add("scalar", lambda sc, dst=dst, psl=psl: sc.copy(dst, psl),
                          reads=[Bp[bank]], writes=[Bat[kc][tt]],
                          tag=tag + "_ev%d_%d" % (kc, tt))

    def transpose_to_at(src_buf_ap, rbufs_of_tile, tag, scale_col=None, bias_col=None):
        for tt in range(NTQ):
            transpose_tile_to_at(tt, src_buf_ap, rbufs_of_tile(tt), tag,
                                 scale_col=scale_col, bias_col=bias_col)

    def conv_steps(q, cj, src_ap, src_bufs, carry_buf, carry_ap, gate_ap_fn, gate_bufs,
                   dst_ap_fn, dst_bufs, tag):
        units = [(i, h) for i in range(NTQ) for h in range(2)]

        def emit_fwd(u):
            i, h = units[u]
            if i == 0:
                prev_ap, prev_b = carry_ap, [carry_buf]
            else:
                prev_ap = src_ap[:, (i - 1) * D: i * D]
                prev_b = src_bufs[i - 1]
            cur_ap = src_ap[:, i * D:(i + 1) * D]
            cur_b = src_bufs[i]
            par = u % 2
            b0, b1 = 2 * par, 2 * par + 1
            prevh = prev_ap[:, h * 512:(h + 1) * 512]
            curh = cur_ap[:, h * 512:(h + 1) * 512]

            def fwd_fn(t, b0=b0, b1=b1, prevh=prevh, curh=curh):
                last = None
                for s_i, bank in ((0, b0), (1, b1)):
                    last = t.matmul(pb[bank][:, :], FWD_HI(s_i, 0), prevh, start=True, stop=False)
                    last = t.matmul(pb[bank][:, :], FWD_HI(s_i, 1), curh, start=False, stop=False)
                    last = t.matmul(pb[bank][:, :], FWD_LO(s_i, 0), prevh, start=False, stop=False)
                    last = t.matmul(pb[bank][:, :], FWD_LO(s_i, 1), curh, start=False, stop=True)
                return last
            S.add("tensor", fwd_fn, reads=[*prev_b, *cur_b, Bsetup],
                  writes=[Bp[b0], Bp[b1]], tag="%s_fwd%d_%d" % (tag, i, h))

        def emit_rest(u):
            i, h = units[u]
            par = u % 2
            b0, b1 = 2 * par, 2 * par + 1
            binv = 4 + par
            A0 = KA(cj, 0)[:, h * 512:(h + 1) * 512]
            A1 = KA(cj, 1)[:, h * 512:(h + 1) * 512]
            w0 = s_wtmp[:, par * 1024:par * 1024 + 512]
            w1 = s_wtmp[:, par * 1024 + 512:par * 1024 + 1024]
            tmp = s_wtmp[:, 2048:2560]

            def pw_fn(v, b0=b0, b1=b1, A0=A0, A1=A1, w0=w0, w1=w1, tmp=tmp):
                v.tensor_tensor(tmp, pb[b1][:, :], A1, Alu.mult)
                v.tensor_tensor(w0, pb[b0][:, :], A0, Alu.mult)
                v.drain()
                v.tensor_tensor(w0, w0, tmp, Alu.subtract)
                v.drain()
                v.tensor_tensor(tmp, pb[b1][:, :], A0, Alu.mult)
                v.tensor_tensor(w1, pb[b0][:, :], A1, Alu.mult)
                v.drain()
                v.tensor_tensor(w1, w1, tmp, Alu.add)
                v.drain()
                v.tensor_tensor(w0[0:1, :], pb[b0][0:1, :], A0[0:1, :], Alu.mult)
                return v.tensor_tensor(w1[0:1, :], pb[b1][0:1, :], A1[0:1, :], Alu.mult)
            S.add("vector", pw_fn, reads=[Bp[b0], Bp[b1], Bksp],
                  writes=[Bwt[par], Bwtmp], tag="%s_pw%d_%d" % (tag, i, h))

            def inv_fn(t, binv=binv, w0=w0, w1=w1):
                t.matmul(pb[binv][:, :], G_HI(0), w0, start=True, stop=False)
                t.matmul(pb[binv][:, :], G_HI(1), w1, start=False, stop=False)
                t.matmul(pb[binv][:, :], G_LO(0), w0, start=False, stop=False)
                return t.matmul(pb[binv][:, :], G_LO(1), w1, start=False, stop=True)
            S.add("tensor", inv_fn, reads=[Bwt[par], Bsetup], writes=[Bp[binv]],
                  tag="%s_inv%d_%d" % (tag, i, h))
            gate_ap = gate_ap_fn(i, h)
            dst_ap = dst_ap_fn(i, h)
            S.add("vector", lambda v, binv=binv, gate_ap=gate_ap, dst_ap=dst_ap:
                  v.tensor_tensor(dst_ap, pb[binv][:, :], gate_ap, Alu.mult),
                  reads=[Bp[binv], gate_bufs(i, h)], writes=[dst_bufs(i, h)],
                  tag="%s_gate%d_%d" % (tag, i, h))

        emit_fwd(0)
        for u in range(1, len(units)):
            emit_fwd(u)
            emit_rest(u - 1)
        emit_rest(len(units) - 1)

    for q in range(NQ):
        r0 = q * TQ
        pipelined = (until != "attn")
        if q == 0 or not pipelined:
            for t in range(NTQ):
                dma("sync", xtile(s_x, t),
                    dx[r0 + t * 128:r0 + (t + 1) * 128, :],
                    writes=[Bx[t]], chan="x%d" % t, tag="q%d_xin%d" % (q, t))
            for t in range(NTQ):
                emit_ln12(q, t, xtile(s_x, t), [Bx[t]])
        else:
            for t in range(NTQ):
                S.add("gpsimd", lambda v, t=t: v.tensor_copy(
                    xtile(s_x, t), s_big[:, t * 1024:(t + 1) * 1024]),
                    reads=[Bbig[2 * t], Bbig[2 * t + 1]], writes=[Bx[t]],
                    tag="q%d_xcp%d" % (q, t))

        if q == 0 or not pipelined:
            transpose_to_at(s_xn, lambda tt: [Bxn[tt]], "q%d_xnT" % q,
                            scale_col=s_gA, bias_col=s_bA)
        if q == 0:
            emit_presplit()

        for ms in range(24):
            slot = ms % 2
            wap = s_wb[:, slot * 1024:(slot + 1) * 1024]
            dma("sync", wap.rearrange("p (kc m) -> p kc m", m=128),
                dw["L0_attn_W_in"][:, ms * 128:(ms + 1) * 128]
                .rearrange("(kc p) m -> p kc m", p=128),
                writes=[Bwb[slot]], chan="wb%d" % slot, tag="q%d_win%d" % (q, ms))
            dma("sync", s_bias[:, slot * 128:(slot + 1) * 128],
                dw["L0_attn_b_in"][ms * 128:(ms + 1) * 128].unsqueeze(0).broadcast_to([128, 128]),
                writes=[Bbias[slot]], chan="b%d" % slot, tag="q%d_binb%d" % (q, ms))
            bank = 4 + ms % 2

            for tt in range(NTQ):
                def mm_fn(t, bank=bank, wap=wap, tt=tt):
                    last = None
                    for kc in range(8):
                        last = t.matmul(
                            pb[bank][:, tt * 128:(tt + 1) * 128],
                            s_at[:, kc * 512 + tt * 128: kc * 512 + (tt + 1) * 128],
                            wap[:, kc * 128:(kc + 1) * 128],
                            start=(tt == 0 and kc == 0),
                            stop=(tt == NTQ - 1 and kc == 7))
                    return last
                S.add("tensor", mm_fn,
                      reads=[Bwb[slot]] + [Bat[kc][tt] for kc in range(8)],
                      writes=[Bp[bank]], tag="q%d_winmm%d_%d" % (q, ms, tt))
            fcol = (ms % 8) * 128
            if ms < 8:
                dst3 = s_v.rearrange("p (t d) -> p t d", d=D)[:, :, fcol:fcol + 128]
                dstbs = list(Bv)
            elif ms < 16:
                dst3 = s_big[:, 0:4096].rearrange("p (t d) -> p t d", d=D)[:, :, fcol:fcol + 128]
                dstbs = [Bbig[2 * tt + (1 if fcol >= 512 else 0)] for tt in range(NTQ)]
            else:
                dst3 = s_big[:, 4096:8192].rearrange("p (t d) -> p t d", d=D)[:, :, fcol:fcol + 128]
                dstbs = [Bbig[8 + 2 * tt + (1 if fcol >= 512 else 0)] for tt in range(NTQ)]
            S.add("vector", lambda v, dst3=dst3, bank=bank, slot=slot:
                  v.tensor_tensor(
                      dst3, pb[bank][:, :].rearrange("p (t c) -> p t c", c=128),
                      s_bias[:, slot * 128:(slot + 1) * 128]
                      .unsqueeze(1).broadcast_to([128, NTQ, 128]), Alu.add),
                  reads=[Bp[bank], Bbias[slot]], writes=dstbs,
                  tag="q%d_winev%d" % (q, ms))

        conv_steps(
            q, 0, s_v, [[Bv[i]] for i in range(4)], Bcry_v, s_cry[:, 0:1024],
            gate_ap_fn=lambda i, h: s_big[:, i * D + h * 512: i * D + (h + 1) * 512],
            gate_bufs=lambda i, h: Bbig[2 * i + h],
            dst_ap_fn=lambda i, h: s_big[:, i * D + h * 512: i * D + (h + 1) * 512],
            dst_bufs=lambda i, h: Bbig[2 * i + h],
            tag="q%d_c1" % q)
        S.add("vector", lambda v: v.tensor_copy(s_cry[:, 0:1024], s_v[:, 3 * D:4 * D]),
              reads=[Bv[3]], writes=[Bcry_v], tag="q%d_vcarry" % q)

        conv_steps(
            q, 1, s_big[:, 0:4096],
            [[Bbig[2 * i], Bbig[2 * i + 1]] for i in range(4)],
            Bcry_y, s_cry[:, 1024:2048],
            gate_ap_fn=lambda i, h: s_big[:, 4096 + i * D + h * 512: 4096 + i * D + (h + 1) * 512],
            gate_bufs=lambda i, h: Bbig[8 + 2 * i + h],
            dst_ap_fn=lambda i, h: s_big[:, 4096 + i * D + h * 512: 4096 + i * D + (h + 1) * 512],
            dst_bufs=lambda i, h: Bbig[8 + 2 * i + h],
            tag="q%d_c2" % q)
        S.add("vector", lambda v: v.tensor_copy(s_cry[:, 1024:2048], s_big[:, 3 * D:4 * D]),
              reads=[Bbig[6], Bbig[7]], writes=[Bcry_y], tag="q%d_ycarry" % q)

        transpose_to_at(s_big[:, 4096:8192], lambda tt: [Bbig[8 + 2 * tt], Bbig[8 + 2 * tt + 1]], "q%d_y2T" % q)
        for ms in range(8):
            slot = ms % 2
            wap = s_wb[:, slot * 1024:(slot + 1) * 1024]
            dma("sync", wap.rearrange("p (kc m) -> p kc m", m=128),
                dw["L0_attn_W_out"][:, ms * 128:(ms + 1) * 128]
                .rearrange("(kc p) m -> p kc m", p=128),
                writes=[Bwb[slot]], chan="wb%d" % slot, tag="q%d_wout%d" % (q, ms))
            dma("sync", s_bias[:, slot * 128:(slot + 1) * 128],
                dw["L0_attn_b_out"][ms * 128:(ms + 1) * 128].unsqueeze(0).broadcast_to([128, 128]),
                writes=[Bbias[slot]], chan="b%d" % slot, tag="q%d_boutb%d" % (q, ms))
            bank = 4 + ms % 2

            for tt in range(NTQ):
                def mm_fn(t, bank=bank, wap=wap, tt=tt):
                    last = None
                    for kc in range(8):
                        last = t.matmul(
                            pb[bank][:, tt * 128:(tt + 1) * 128],
                            s_at[:, kc * 512 + tt * 128: kc * 512 + (tt + 1) * 128],
                            wap[:, kc * 128:(kc + 1) * 128],
                            start=(tt == 0 and kc == 0),
                            stop=(tt == NTQ - 1 and kc == 7))
                    return last
                S.add("tensor", mm_fn,
                      reads=[Bwb[slot]] + [Bat[kc][tt] for kc in range(8)],
                      writes=[Bp[bank]], tag="q%d_woutmm%d_%d" % (q, ms, tt))
            fcol = ms * 128
            dst3 = s_xn.rearrange("p (t d) -> p t d", d=D)[:, :, fcol:fcol + 128]
            z3 = s_z.rearrange("p (t d) -> p t d", d=D)[:, :, fcol:fcol + 128]

            def wev_fn(v, dst3=dst3, z3=z3, bank=bank, slot=slot):
                v.tensor_tensor(
                    dst3, pb[bank][:, :].rearrange("p (t c) -> p t c", c=128),
                    s_bias[:, slot * 128:(slot + 1) * 128]
                    .unsqueeze(1).broadcast_to([128, NTQ, 128]), Alu.add)
                v.drain()
                return v.tensor_tensor(dst3, dst3, z3, Alu.add)
            S.add("vector", wev_fn, reads=[Bp[bank], Bbias[slot], *Bz],
                  writes=list(Bxn), tag="q%d_woutev%d" % (q, ms))

        for t in range(NTQ):
            ln_steps(xtile(s_xn, t), xtile(s_z, t), [Bxn[t]], [Bz[t]], "q%d_ln3_%d" % (q, t), slot=t)
            S.add("gpsimd", lambda v, t=t: v.tensor_tensor(
                xtile(s_x, t), xtile(s_x, t), xtile(s_z, t), Alu.add),
                reads=[Bx[t], Bz[t]], writes=[Bx[t]], tag="q%d_res1_%d" % (q, t))

        if until == "attn":
            for t in range(NTQ):
                dma("sync", dy[r0 + t * 128:r0 + (t + 1) * 128, :], xtile(s_x, t),
                    reads=[Bx[t]], chan="x%d" % t, tag="q%d_xout%d" % (q, t))
            continue
        for t in range(NTQ):
            ln_steps(xtile(s_x, t), xtile(s_z, t), [Bx[t]], [Bz[t]], "q%d_ln4_%d" % (q, t), slot=t)
        transpose_to_at(s_z, lambda tt: [Bz[tt]], "q%d_xfT" % q)

        of_init = S.add("vector", lambda v: v.memset(s_of[:, :], 0.0),
                        writes=[Bof], tag="q%d_ofinit" % q)
        for g in range(4):
            for kc in range(8):
                slot = kc % 2
                eap = s_wb[:, slot * 1024: slot * 1024 + 512]
                dma("sync", eap,
                    dw["L0_ffn_enc_W"][kc * 128:(kc + 1) * 128, g * 512:(g + 1) * 512],
                    writes=[Bwb[slot]], chan="wb%d" % slot, tag="q%d_enc%d_%d" % (q, g, kc))
                def mm_fn(t, g=g, kc=kc, eap=eap):
                    last = None
                    for mi in range(4):
                        last = t.matmul(pb[mi][:, :],
                                        eap[:, mi * 128:(mi + 1) * 128],
                                        s_at[:, kc * 512:(kc + 1) * 512],
                                        start=(kc == 0), stop=(kc == 7))
                    return last
                S.add("tensor", mm_fn, reads=[Bwb[slot]] + Bat[kc],
                      writes=[Bp[0], Bp[1], Bp[2], Bp[3]],
                      tag="q%d_encmm%d_%d" % (q, g, kc))
            for mi in range(4):
                mc = g * 4 + mi
                S.add("scalar", lambda sc, mc=mc, mi=mi: sc.activation(
                    s_big[:, mc * 512:(mc + 1) * 512], pb[mi][:, :], Act.Sigmoid,
                    bias=s_encb[:, mc:mc + 1]),
                    reads=[Bp[mi], Bsetup], writes=[Bbig[mc]],
                    tag="q%d_encev%d" % (q, mc))
                S.add("tensor", lambda t, mc=mc: t.matmul(
                    pb[6][0:64, :], s_s2c[:, mc * 64:(mc + 1) * 64],
                    s_big[:, mc * 512:(mc + 1) * 512],
                    start=(mc == 0), stop=(mc == 15)),
                    reads=[Bbig[mc], Bsetup], writes=[Bp[6]],
                    tag="q%d_s2c%d" % (q, mc))
        S.add("scalar", lambda sc: sc.activation(
            s_cont[:, :], pb[6][0:64, :], Act.Identity, bias=s_s2cb[:, 0:1]),
            reads=[Bp[6], Bsetup], writes=[Bcont], tag="q%d_contev" % q)

        def csp_fn(v):
            v.tensor_copy(s_conth[0:64, :], s_cont[:, :])
            v.drain()
            return v.tensor_tensor(s_contl[0:64, :], s_cont[:, :], s_conth[0:64, :],
                                   Alu.subtract)
        S.add("vector", csp_fn, reads=[Bcont], writes=[Bcsp], tag="q%d_contsp" % q)
        dma("gpsimd", s_conth[64:128, :], s_conth[0:64, :], reads=[Bcsp],
            writes=[Bcsp], chan="dup", tag="q%d_cdup_h" % q)
        dma("gpsimd", s_contl[64:128, :], s_contl[0:64, :], reads=[Bcsp],
            writes=[Bcsp], chan="dup", tag="q%d_cdup_l" % q)

        S.add("tensor", lambda t: t.matmul(pb[7][0:64, :], s_rw1[:, :], s_cont[:, :],
                                           start=True, stop=True),
              reads=[Bcont, Bsetup], writes=[Bp[7]], tag="q%d_rw1" % q)
        S.add("scalar", lambda sc: sc.activation(s_eo[:, :], pb[7][0:64, :], Act.Tanh,
                                                 bias=s_rb1[:, 0:1]),
              reads=[Bp[7], Bsetup], writes=[Beo], tag="q%d_tanh" % q)
        S.add("tensor", lambda t: t.matmul(pb[6][0:8, :], s_rw2[:, :], s_eo[:, :],
                                           start=True, stop=True),
              reads=[Beo, Bsetup], writes=[Bp[6]], tag="q%d_rw2" % q)
        S.add("scalar", lambda sc: sc.activation(s_lg[0:8, 0:512], pb[6][0:8, :],
                                                 Act.Identity, bias=s_rb2[:, 0:1]),
              reads=[Bp[6], Bsetup], writes=[Blg], tag="q%d_logev" % q)
        def expert_mm(e):
            eslot = e % 2
            w1h = s_we1h[:, eslot * 1024:(eslot + 1) * 1024]
            w1l = s_we1l[:, eslot * 1024:(eslot + 1) * 1024]
            w2h = s_we2h[:, eslot * 512:(eslot + 1) * 512]
            w2l = s_we2l[:, eslot * 512:(eslot + 1) * 512]
            dma("sync", w1h, dwe1h[e, :, :], reads=[Bpw1], writes=[Bwe1[eslot]],
                chan="we1_%d" % eslot, tag="q%d_we1h_%d" % (q, e))
            dma("gpsimd", w1l, dwe1l[e, :, :], reads=[Bpw1], writes=[Bwe1[eslot]],
                chan="wl1_%d" % eslot, tag="q%d_we1l_%d" % (q, e))
            dma("sync", w2h.rearrange("p (hc m) -> p hc m", m=64),
                dwe2h[e, :, :].rearrange("(hc p) m -> p hc m", p=128),
                reads=[Bpw2], writes=[Bwe2s[eslot]], chan="we2h%d" % eslot,
                tag="q%d_we2h_%d" % (q, e))
            dma("gpsimd", w2l.rearrange("p (hc m) -> p hc m", m=64),
                dwe2l[e, :, :].rearrange("(hc p) m -> p hc m", p=128),
                reads=[Bpw2], writes=[Bwe2s[eslot]], chan="we2l%d" % eslot,
                tag="q%d_we2l_%d" % (q, e))

            def we1_pair(j):
                for half in range(2):
                    mc = 2 * j + half
                    hb = 2 * (j % 2) + half
                    bank = (j % 2) * 2 + half
                    p0, p1 = 64 * half, 64 * (half + 1)
                    S.add("tensor", lambda t, mc=mc, bank=bank, p0=p0, p1=p1,
                          w1h=w1h, w1l=w1l: (
                        t.matmul(pb[bank][:, :],
                                 w1h[p0:p1, mc * 128:(mc + 1) * 128],
                                 s_conth[p0:p1, :], start=True, stop=False,
                                 tile_position=(p0, 0)),
                        t.matmul(pb[bank][:, :],
                                 w1h[p0:p1, mc * 128:(mc + 1) * 128],
                                 s_contl[p0:p1, :], start=False, stop=False,
                                 tile_position=(p0, 0)),
                        t.matmul(pb[bank][:, :],
                                 w1l[p0:p1, mc * 128:(mc + 1) * 128],
                                 s_conth[p0:p1, :], start=False, stop=True,
                                 tile_position=(p0, 0)))[-1],
                        reads=[Bwe1[eslot], Bcsp], writes=[Bp[bank]],
                        tag="q%d_we1mm%d_%d" % (q, e, mc))
                    t32 = s_wtmp[:, half * 1024: half * 1024 + 512]

                    def hrl_fn(sc, mc=mc, hb=hb, e=e, t32=t32, bank=bank):
                        sc.activation(t32, pb[bank][:, :], Act.Relu,
                                      bias=s_be1[:, e * 8 + mc:e * 8 + mc + 1])
                        sc.drain()
                        return sc.copy(s_hh[:, hb * 512:(hb + 1) * 512], t32)
                    S.add("scalar", hrl_fn, reads=[Bp[bank], Bsetup],
                          writes=[Bhh[hb], Bwt[half]], tag="q%d_hrl%d_%d" % (q, e, mc))
                    S.add("vector", lambda v, hb=hb, t32=t32: v.tensor_tensor(
                        s_hl[:, hb * 512:(hb + 1) * 512], t32,
                        s_hh[:, hb * 512:(hb + 1) * 512], Alu.subtract),
                        reads=[Bhh[hb], Bwt[half]], writes=[Bhl[hb]],
                        tag="q%d_hsb%d_%d" % (q, e, mc))

            def we2_mm(mc):
                hb = 2 * ((mc // 2) % 2) + mc % 2
                bk = 6 if e % 2 == 0 else 4
                S.add("tensor", lambda t, mc=mc, hb=hb, bk=bk: (
                    t.matmul(pb[bk][0:64, :], w2h[:, mc * 64:(mc + 1) * 64],
                             s_hh[:, hb * 512:(hb + 1) * 512],
                             start=(mc == 0), stop=False),
                    t.matmul(pb[bk][0:64, :], w2l[:, mc * 64:(mc + 1) * 64],
                             s_hh[:, hb * 512:(hb + 1) * 512],
                             start=False, stop=False))[-1],
                    reads=[Bhh[hb], Bwe2s[eslot]], writes=[Bp[bk]],
                    tag="q%d_we2mma%d_%d" % (q, e, mc))
                S.add("tensor", lambda t, mc=mc, hb=hb, bk=bk: t.matmul(
                    pb[bk][0:64, :], w2h[:, mc * 64:(mc + 1) * 64],
                    s_hl[:, hb * 512:(hb + 1) * 512],
                    start=False, stop=(mc == 7)),
                    reads=[Bhl[hb], Bwe2s[eslot]], writes=[Bp[bk]],
                    tag="q%d_we2mmb%d_%d" % (q, e, mc))

            we1_pair(0)
            for j in range(1, 4):
                we1_pair(j)
                we2_mm(2 * j - 2)
                we2_mm(2 * j - 1)
            we2_mm(6)
            we2_mm(7)

        def expert_gate(e):
                S.add("tensor", lambda t, e=e: t.matmul(
                    pb[7][0:64, :], s_sel[:, e * 64:(e + 1) * 64], s_lg[0:8, :],
                    start=True, stop=True),
                    reads=[Blg, Bsetup], writes=[Bp[7]], tag="q%d_gbc%d" % (q, e))

                bk = 6 if e % 2 == 0 else 4
                S.add("vector", lambda v, e=e, bk=bk: v.tensor_scalar(
                    s_eo[:, :], pb[bk][0:64, :], s_be2[:, e:e + 1], None, Alu.add),
                    reads=[Bp[bk], Bsetup], writes=[Beo], tag="q%d_eoev%d" % (q, e))
                S.add("vector", lambda v: v.tensor_tensor(s_eo[:, :], s_eo[:, :],
                                                          pb[7][0:64, :], Alu.mult),
                      reads=[Beo, Bp[7]], writes=[Beo], tag="q%d_eog%d" % (q, e))
                S.add("vector", lambda v: v.tensor_tensor(s_of[:, :], s_of[:, :], s_eo[:, :],
                                                          Alu.add),
                      reads=[Bof, Beo], writes=[Bof], tag="q%d_ofacc%d" % (q, e))

        expert_mm(0)


        def lgt_fn(t):
            last = None
            for tt in range(NTQ):
                last = t.transpose(pb[7][:, tt * 8:(tt + 1) * 8],
                                   s_lg[0:8, tt * 128:(tt + 1) * 128], IDENT[0:8, 0:8])
            return last
        S.add("tensor", lgt_fn, reads=[Blg, Bsetup], writes=[Bp[7]], tag="q%d_lgT" % q)
        S.add("vector", lambda v: v.tensor_copy(s_gtm[:, :], pb[7][:, 0:32]),
              reads=[Bp[7]], writes=[Bgtm], tag="q%d_lgTev" % q)
        lg3 = s_gtm[:, 0:32].rearrange("p (t e) -> p t e", e=8)
        pr3 = s_gw[:, 0:32].rearrange("p (t e) -> p t e", e=8)
        mk3 = s_gw[:, 32:64].rearrange("p (t e) -> p t e", e=8)
        NEG4 = s_st[:, 64:68]
        SUM4 = s_st[:, 68:72]
        M1_4 = s_st[:, 72:76]
        M2_4 = s_st[:, 76:80]
        S12_4 = s_st[:, 80:84]
        REC4 = s_st[:, 84:88]
        ZCOL = s_st[:, 129:130]
        bc = lambda a: a.unsqueeze(2).broadcast_to([128, 4, 8])
        S.add("vector", lambda v: v.tensor_reduce(NEG4, lg3, AX, Alu.max, negate=True),
              reads=[Bgtm], writes=[Bgst[0]], tag="q%d_gm" % q)
        S.add("vector", lambda v: v.tensor_tensor(lg3, lg3, bc(NEG4), Alu.add),
              reads=[Bgtm, Bgst[0]], writes=[Bgtm], tag="q%d_gsub" % q)
        S.add("scalar", lambda sc: sc.activation(s_gw[:, 0:32], s_gtm[:, 0:32],
                                                 Act.Exp, bias=ZCOL),
              reads=[Bgtm, Bsetup], writes=[Bgw], tag="q%d_gexp" % q)
        S.add("vector", lambda v: v.tensor_reduce(SUM4, pr3, AX, Alu.add),
              reads=[Bgw], writes=[Bgst[0]], tag="q%d_gsum" % q)
        S.add("vector", lambda v: v.reciprocal(REC4, SUM4),
              reads=[Bgst[0]], writes=[Bgst[0]], tag="q%d_grec" % q)
        S.add("vector", lambda v: v.tensor_tensor(pr3, pr3, bc(REC4), Alu.mult),
              reads=[Bgw, Bgst[0]], writes=[Bgw], tag="q%d_gnorm" % q)
        S.add("vector", lambda v: v.tensor_reduce(M1_4, pr3, AX, Alu.max),
              reads=[Bgw], writes=[Bgst[0]], tag="q%d_gm1" % q)
        S.add("vector", lambda v: v.tensor_tensor(mk3, pr3, bc(M1_4), Alu.is_lt),
              reads=[Bgw, Bgst[0]], writes=[Bgw], tag="q%d_gmask" % q)
        S.add("vector", lambda v: v.tensor_tensor(mk3, mk3, pr3, Alu.mult),
              reads=[Bgw], writes=[Bgw], tag="q%d_gmask2" % q)
        S.add("vector", lambda v: v.tensor_reduce(M2_4, mk3, AX, Alu.max),
              reads=[Bgw], writes=[Bgst[0]], tag="q%d_gm2" % q)
        S.add("vector", lambda v: v.tensor_tensor(S12_4, M1_4, M2_4, Alu.add),
              reads=[Bgst[0]], writes=[Bgst[0]], tag="q%d_gs12" % q)
        S.add("vector", lambda v: v.reciprocal(REC4, S12_4),
              reads=[Bgst[0]], writes=[Bgst[0]], tag="q%d_grec2" % q)
        S.add("vector", lambda v: v.tensor_tensor(mk3, pr3, bc(M2_4), Alu.is_ge),
              reads=[Bgw, Bgst[0]], writes=[Bgw], tag="q%d_gkeep" % q)
        S.add("vector", lambda v: v.tensor_tensor(pr3, pr3, bc(REC4), Alu.mult),
              reads=[Bgw, Bgst[0]], writes=[Bgw], tag="q%d_gsc" % q)
        S.add("vector", lambda v: v.tensor_tensor(lg3, pr3, mk3, Alu.mult),
              reads=[Bgw], writes=[Bgtm], tag="q%d_gfin" % q)
        def gT_fn(t):
            last = None
            for tt in range(NTQ):
                last = t.transpose(pb[7][0:8, tt * 128:(tt + 1) * 128],
                                   s_gtm[:, tt * 8:(tt + 1) * 8], IDENT)
            return last
        expert_mm(1)
        S.add("tensor", gT_fn, reads=[Bgtm, Bsetup], writes=[Bp[7]], tag="q%d_gT" % q)
        S.add("vector", lambda v: v.tensor_copy(s_lg[0:8, :], pb[7][0:8, :]),
              reads=[Bp[7]], writes=[Blg], tag="q%d_gTev" % q)
        expert_gate(0)
        for e in range(2, 8):
            expert_mm(e)
            expert_gate(e - 1)
        expert_gate(7)

        for half in range(2):
            dma("sync", s_wb[0:64, half * 1024:(half + 1) * 1024],
                dw["L0_ffn_c2s_W"][:, half * 1024:(half + 1) * 1024],
                writes=[Bwb[half]], chan="wb%d" % half, tag="q%d_c2sw%d" % (q, half))
        for mc in range(16):
            bank = 2 + mc % 2
            half = mc // 8
            S.add("tensor", lambda t, mc=mc, bank=bank, half=half: t.matmul(
                pb[bank][:, :],
                s_wb[0:64, half * 1024 + (mc % 8) * 128: half * 1024 + (mc % 8 + 1) * 128],
                s_of[:, :], start=True, stop=True),
                reads=[Bof, Bwb[half]], writes=[Bp[bank]], tag="q%d_c2s%d" % (q, mc))
            S.add("scalar", lambda sc, mc=mc, bank=bank: sc.activation(
                s_big[:, mc * 512:(mc + 1) * 512], pb[bank][:, :], Act.Sigmoid,
                bias=s_c2sb[:, mc:mc + 1]),
                reads=[Bp[bank], Bsetup], writes=[Bbig[mc]], tag="q%d_c2sev%d" % (q, mc))

        for g2 in range(2):
            for kc in range(16):
                slot = kc % 2
                dap = s_wb[:, slot * 1024: slot * 1024 + 512]
                dma("sync", dap,
                    dw["L0_ffn_dec_W"][kc * 128:(kc + 1) * 128, g2 * 512:(g2 + 1) * 512],
                    writes=[Bwb[slot]], chan="wb%d" % slot, tag="q%d_dec%d_%d" % (q, g2, kc))
                def mm_fn(t, g2=g2, kc=kc, dap=dap):
                    last = None
                    for mi in range(4):
                        last = t.matmul(pb[mi][:, :],
                                        dap[:, mi * 128:(mi + 1) * 128],
                                        s_big[:, kc * 512:(kc + 1) * 512],
                                        start=(kc == 0), stop=(kc == 15))
                    return last
                S.add("tensor", mm_fn, reads=[Bwb[slot], Bbig[kc]],
                      writes=[Bp[0], Bp[1], Bp[2], Bp[3]],
                      tag="q%d_decmm%d_%d" % (q, g2, kc))
            for mi in range(4):
                m = g2 * 4 + mi
                S.add("scalar", lambda sc, m=m, mi=mi: sc.activation(
                    s_xn[:, m * 512:(m + 1) * 512], pb[mi][:, :], Act.Sigmoid,
                    bias=s_decb[:, m:m + 1]),
                    reads=[Bp[mi], Bsetup], writes=[Bxn[m // 2]],
                    tag="q%d_decev%d" % (q, m))

        if pipelined and q + 1 < NQ:
            r1_ = (q + 1) * TQ
            for t in range(NTQ):
                dma("sync", s_big[:, t * 1024:(t + 1) * 1024],
                    dx[r1_ + t * 128:r1_ + (t + 1) * 128, :],
                    writes=[Bbig[2 * t], Bbig[2 * t + 1]], chan="x%d" % t,
                    tag="q%d_xin%d" % (q + 1, t))

        for m in range(8):
            bank = 6 + m % 2

            def tp_fn(t, m=m, bank=bank):
                last = None
                for tt in range(NTQ):
                    last = t.transpose(pb[bank][:, tt * 128:(tt + 1) * 128],
                                       s_xn[:, m * 512 + tt * 128: m * 512 + (tt + 1) * 128],
                                       IDENT)
                return last
            S.add("tensor", tp_fn, reads=[*Bxn, Bsetup], writes=[Bp[bank]],
                  tag="q%d_decT%d" % (q, m))

            if m % 2 == 0:
                def ev_fn(v, m=m, bank=bank):
                    last = None
                    for tt in range(NTQ):
                        last = v.tensor_copy(
                            s_z[:, tt * D + m * 128: tt * D + (m + 1) * 128],
                            pb[bank][:, tt * 128:(tt + 1) * 128])
                    return last
            else:
                def ev_fn(sc, m=m, bank=bank):
                    last = None
                    for tt in range(NTQ):
                        if tt:
                            sc.drain()
                        last = sc.copy(
                            s_z[:, tt * D + m * 128: tt * D + (m + 1) * 128],
                            pb[bank][:, tt * 128:(tt + 1) * 128])
                    return last
            S.add("vector" if m % 2 == 0 else "scalar", ev_fn,
                  reads=[Bp[bank]], writes=list(Bz),
                  tag="q%d_decTev%d" % (q, m))
        P8 = pipelined and q + 1 < NQ
        bt8 = lambda t: s_big[:, t * 1024:(t + 1) * 1024]
        br8 = lambda t: [Bbig[2 * t], Bbig[2 * t + 1]]
        for st_ in ("sa", "sq", "rc", "ap"):
            for t in range(NTQ):
                ln_steps(xtile(s_z, t), xtile(s_v, t), [Bz[t]], [Bv[t]],
                         "q%d_ln5_%d" % (q, t), slot=t, stage=st_)
                if P8:
                    emit_ln12(q + 1, t, bt8(t), br8(t), stage=st_)
        if P8:
            for t in range(NTQ):
                transpose_tile_to_at(t, s_xn, [Bxn[t]], "q%d_xnT" % (q + 1),
                                     scale_col=s_gA, bias_col=s_bA)
        for t in range(NTQ):
            eng_gb = "gpsimd" if t % 2 == 0 else "vector"
            S.add(eng_gb, lambda v, t=t: v.tensor_tensor(
                xtile(s_v, t), xtile(s_v, t), s_gb[:, 0:1024], Alu.mult),
                reads=[Bv[t], Bgb], writes=[Bv[t]], tag="q%d_lng_%d" % (q, t))
            S.add(eng_gb, lambda v, t=t: v.tensor_tensor(
                xtile(s_v, t), xtile(s_v, t), s_gb[:, 1024:2048], Alu.add),
                reads=[Bv[t], Bgb], writes=[Bv[t]], tag="q%d_lnb_%d" % (q, t))
        for st_ in ("sa", "sq", "rc", "ap"):
            for t in range(NTQ):
                ln_steps(xtile(s_v, t), xtile(s_v, t), [Bv[t]], [Bv[t]],
                         "q%d_ln6_%d" % (q, t), slot=t, stage=st_)
        for t in range(NTQ):
            S.add("gpsimd" if t % 2 == 0 else "vector", lambda v, t=t: v.tensor_tensor(
                xtile(s_x, t), xtile(s_x, t), xtile(s_v, t), Alu.add),
                reads=[Bx[t], Bv[t]], writes=[Bx[t]],
                tag="q%d_res2_%d" % (q, t))

        for t in range(NTQ):
            dma("sync", dy[r0 + t * 128:r0 + (t + 1) * 128, :], xtile(s_x, t),
                reads=[Bx[t]], chan="x%d" % t, tag="q%d_xout%d" % (q, t))

    out_steps = [st for st in S.steps if "_xout" in st.tag]
    S.add("sync", lambda e: e.nop(), deps=out_steps, tag="fence")
    return nc, S


def _dma_instr_counts(nc, S):
    per_engine_steps = {}
    for st in S.steps:
        if st.is_dma:
            per_engine_steps.setdefault(st.engine, []).append(st)
    eng_map = {"sync": "SP", "gpsimd": "Pool", "scalar": "Activation",
               "vector": "DVE", "tensor": "PE"}
    counts = {}
    for blkm in nc.m.functions[0].blocks:
        insts = list(blkm.instructions)
        if not insts:
            continue
        name = str(insts[0].engine).split(".")[-1]
        steps = None
        for k, v in eng_map.items():
            if v == name:
                steps = per_engine_steps.get(k)
        if not steps:
            continue
        idx = 0
        run = 0
        for i in insts:
            nm = type(i).__name__
            if nm == "InstDMACopy":
                run += 1
            elif nm == "InstNoOp" and run > 0:
                counts[steps[idx].i] = run
                idx += 1
                run = 0
        assert idx == len(steps), "dma walk mismatch %s: %d vs %d" % (name, idx, len(steps))
    return counts


def build_nc(T, until="full"):
    nc0, S0 = _build(T, until)
    with nc0.Block() as blk0:
        S0.emit(nc0, blk0, count_mode=True)
    counts = _dma_instr_counts(nc0, S0)
    nc, S = _build(T, until)
    weights = {}
    for st0, st in zip(S0.steps, S.steps):
        if st.is_dma and st0.i in counts:
            weights[st.i] = counts[st0.i]
    with nc.Block() as blk:
        S.emit(nc, blk, dma_weights=weights)
    return nc


_CACHE = {}


def kernel(**inputs):
    x = np.ascontiguousarray(np.asarray(inputs["L0_attn_x"], dtype=np.float32))
    B, T, D_ = x.shape
    assert D_ == D and B == 8
    nc = build_nc(T)
    consts = make_consts()
    base = {k: np.ascontiguousarray(np.asarray(inputs[k], dtype=np.float32))
            for k in WEIGHTS}
    base.update(consts)
    in_maps = [dict(base, x=x[i]) for i in range(B)]
    res = run_bass_kernel_spmd(nc, in_maps, core_ids=list(range(B)))
    return np.stack([np.asarray(res.results[i]["y"], dtype=np.float32)
                     for i in range(B)], axis=0)


# version 36 (final)
# speedup vs baseline: 1.0218x; 1.0218x over previous
import math
import numpy as np

import concourse.bass as bass
import concourse.mybir as mybir
from concourse.bass_utils import run_bass_kernel_spmd

F32 = mybir.dt.float32
BF16 = mybir.dt.bfloat16
Alu = mybir.AluOpType
Act = mybir.ActivationFunctionType
AX = mybir.AxisListType.X

D = 1024
CH = 128
TQ = 512
NTQ = TQ // CH


class Step:
    __slots__ = ("i", "engine", "fn", "deps", "is_dma", "chan", "inc", "target",
                 "target_pfx", "chan_before", "tag")

    def __init__(self, i, engine, fn, deps, is_dma, chan, tag):
        self.i, self.engine, self.fn = i, engine, fn
        self.deps, self.is_dma, self.tag = deps, is_dma, tag
        self.chan = chan
        self.inc = False
        self.target = 0
        self.target_pfx = 0
        self.chan_before = 0


class Buf:
    __slots__ = ("name", "writers", "readers")

    def __init__(self, name):
        self.name = name
        self.writers = []
        self.readers = []


class Sched:
    def __init__(self):
        self.steps = []

    def add(self, engine, fn, reads=(), writes=(), deps=(), is_dma=False, chan=None, tag=""):
        dep = []
        seen = set()

        def _push(d):
            if d is not None and id(d) not in seen:
                seen.add(id(d))
                dep.append(d)

        for b in reads:
            for d in b.writers:
                _push(d)
        for b in writes:
            for d in b.writers:
                _push(d)
            for d in b.readers:
                _push(d)
        for d in deps:
            _push(d)
        st = Step(len(self.steps), engine, fn, dep, is_dma, chan, tag)
        for b in reads:
            b.readers.append(st)
        for b in writes:
            if b.readers:
                b.writers = [st]
                b.readers = []
            else:
                b.writers.append(st)
        st.deps = [d for d in st.deps if d is not st]
        self.steps.append(st)
        return st

    BURST = ("gsetup",)

    def emit(self, nc, blk, dma_weights=None, count_mode=False):
        for st in self.steps:
            if st.is_dma:
                st.inc = True
                assert st.chan, "dma step %s needs a channel" % st.tag
            for d in st.deps:
                if d.engine != st.engine or d.is_dma:
                    d.inc = True
        def skey(st):
            return ("D", st.chan) if st.is_dma else (st.engine, False)
        cnt = {}
        for st in self.steps:
            key = skey(st)
            if st.inc:
                w = 1
                if st.is_dma:
                    w = 16 * (dma_weights.get(st.i, 1) if dma_weights else 1)
                st.chan_before = cnt.get(key, 0)
                cnt[key] = cnt.get(key, 0) + w
                st.target = st.target_pfx = cnt[key]
        for st in self.steps:
            if st.is_dma and st.chan in self.BURST:
                st.target = cnt[("D", st.chan)]
        sems = {}
        for key in cnt:
            sems[key] = nc.alloc_semaphore("sem_%s_%s" % (key[0], key[1]))
        mysteps = {}
        for st in self.steps:
            mysteps.setdefault(st.engine, []).append(st)

        def section(engine_name):
            def run(eng):
                waited = {}
                local_idx = {}
                drained_upto = -1
                nloc = 0
                for st in mysteps.get(engine_name, ()):
                    need_drain = False
                    waits = {}
                    for d in st.deps:
                        if d.engine == engine_name and not d.is_dma:
                            if local_idx[id(d)] > drained_upto:
                                need_drain = True
                        elif st.is_dma and d.is_dma and d.chan == st.chan:
                            key = ("D", d.chan)
                            if st.chan_before > waited.get(key, 0):
                                waits[key] = max(waits.get(key, 0), st.chan_before)
                        else:
                            key = ("D", d.chan) if d.is_dma else (d.engine, False)
                            if d.target > waited.get(key, 0):
                                waits[key] = max(waits.get(key, 0), d.target)
                    if need_drain and engine_name in ("vector", "scalar", "gpsimd"):
                        eng.drain()
                        drained_upto = nloc - 1
                    for key, tgt in waits.items():
                        eng.wait_ge(sems[key], tgt)
                        waited[key] = tgt
                    inst = st.fn(eng)
                    if st.inc:
                        assert inst is not None, "step %s needs inc but returned None" % st.tag
                        inst.then_inc(sems[skey(st)], 16 if st.is_dma else 1)
                    if count_mode and st.is_dma:
                        eng.nop()
                    local_idx[id(st)] = nloc
                    nloc += 1
            return run

        blk.tensor(section("tensor"))
        blk.vector(section("vector"))
        blk.scalar(section("scalar"))
        blk.gpsimd(section("gpsimd"))
        blk.sync(section("sync"))


def make_consts():
    n = np.arange(256, dtype=np.float64)
    jj = np.arange(128, dtype=np.float64)
    F = np.zeros((2, 256, 128), np.float64)
    F[0] = np.cos(2 * np.pi * np.outer(n, jj) / 256.0)
    F[1, :, 0] = np.cos(np.pi * n)
    F[1, :, 1:] = -np.sin(2 * np.pi * np.outer(n, np.arange(1, 128, dtype=np.float64)) / 256.0)
    G = np.zeros((256, 128), np.float64)
    m = 128.0 + jj
    G[0, :] = 1.0 / 256.0
    p0 = np.arange(1, 128, dtype=np.float64)
    G[1:128, :] = (2.0 / 256.0) * np.cos(2 * np.pi * np.outer(p0, m) / 256.0)
    G[128, :] = ((-1.0) ** np.arange(128)) / 256.0
    G[129:, :] = (-2.0 / 256.0) * np.sin(2 * np.pi * np.outer(p0, m) / 256.0)

    def hi(a):
        return np.ascontiguousarray(a.astype(np.float32))

    def lo(a):
        return np.ascontiguousarray((a - a.astype(np.float32).astype(np.float64)).astype(np.float32))

    return {
        "cFhi": hi(F), "cFlo": lo(F),
        "cGhi": hi(G), "cGlo": lo(G),
        "cI": np.eye(128, dtype=np.float32),
        "cSel": np.repeat(np.eye(8, dtype=np.float32), 64, axis=1),
    }


WEIGHTS = [
    "L0_attn_W_in", "L0_attn_b_in", "L0_attn_k0", "L0_attn_k1",
    "L0_attn_gamma", "L0_attn_beta", "L0_attn_W_out", "L0_attn_b_out",
    "L0_ffn_enc_W", "L0_ffn_enc_b", "L0_ffn_s2c_W", "L0_ffn_s2c_b",
    "L0_ffn_rW1", "L0_ffn_rb1", "L0_ffn_rW2", "L0_ffn_rb2",
    "L0_ffn_We1", "L0_ffn_be1", "L0_ffn_We2", "L0_ffn_be2",
    "L0_ffn_c2s_W", "L0_ffn_c2s_b", "L0_ffn_dec_W", "L0_ffn_dec_b",
    "L0_ffn_ln_g", "L0_ffn_ln_b",
]


def _build(T, until="full"):
    NQ = T // TQ
    NCHK = T // CH
    nc = bass.Bass("TRN2", target_bir_lowering=False)

    dx = nc.dram_tensor("x", [T, D], F32, kind="ExternalInput").ap()
    dy = nc.dram_tensor("y", [T, D], F32, kind="ExternalOutput").ap()
    _wshapes = [
              ("L0_attn_W_in", [D, 3 * D]), ("L0_attn_b_in", [3 * D]),
              ("L0_attn_k0", [1, D, 128]), ("L0_attn_k1", [1, D, 128]),
              ("L0_attn_gamma", [D]), ("L0_attn_beta", [D]),
              ("L0_attn_W_out", [D, D]), ("L0_attn_b_out", [D]),
              ("L0_ffn_enc_W", [D, 2048]), ("L0_ffn_enc_b", [2048]),
              ("L0_ffn_s2c_W", [2048, 64]), ("L0_ffn_s2c_b", [64]),
              ("L0_ffn_rW1", [64, 64]), ("L0_ffn_rb1", [64]),
              ("L0_ffn_rW2", [64, 8]), ("L0_ffn_rb2", [8]),
              ("L0_ffn_We1", [8, 64, D]), ("L0_ffn_be1", [8, D]),
              ("L0_ffn_We2", [8, D, 64]), ("L0_ffn_be2", [8, 64]),
              ("L0_ffn_c2s_W", [64, 2048]), ("L0_ffn_c2s_b", [2048]),
              ("L0_ffn_dec_W", [2048, D]), ("L0_ffn_dec_b", [D]),
              ("L0_ffn_ln_g", [D]), ("L0_ffn_ln_b", [D]),
          ]
    dw = {n_: nc.dram_tensor(n_, s, F32, kind="ExternalInput").ap()
          for n_, s in _wshapes}
    dcFhi = nc.dram_tensor("cFhi", [2, 256, 128], F32, kind="ExternalInput").ap()
    dcFlo = nc.dram_tensor("cFlo", [2, 256, 128], F32, kind="ExternalInput").ap()
    dcGhi = nc.dram_tensor("cGhi", [256, 128], F32, kind="ExternalInput").ap()
    dcGlo = nc.dram_tensor("cGlo", [256, 128], F32, kind="ExternalInput").ap()
    dcI = nc.dram_tensor("cI", [128, 128], F32, kind="ExternalInput").ap()
    dcSel = nc.dram_tensor("cSel", [8, 512], F32, kind="ExternalInput").ap()
    dwe1h = nc.dram_tensor("we1hd", [8, 128, 1024], BF16, kind="Internal").ap()
    dwe1l = nc.dram_tensor("we1ld", [8, 128, 1024], BF16, kind="Internal").ap()
    dwe2h = nc.dram_tensor("we2hd", [8, 1024, 64], BF16, kind="Internal").ap()
    dwe2l = nc.dram_tensor("we2ld", [8, 1024, 64], BF16, kind="Internal").ap()

    def sb(name, p, cols, dt=F32):
        return nc.alloc_sbuf_tensor(name, [p, cols], dt).ap()

    s_x = sb("s_x", 128, 4096)
    s_z = sb("s_z", 128, 4096)
    s_xn = sb("s_xn", 128, 4096)
    s_at = sb("s_at", 128, 4096)
    s_v = sb("s_v", 128, 4096)
    s_big = sb("s_big", 128, 8192)
    s_wb = sb("s_wb", 128, 2048)
    s_cry = sb("s_cry", 128, 2048)
    s_ksp = sb("s_ksp", 128, 4096)
    s_cons = sb("s_cons", 128, 13 * 128)
    s_gb = sb("s_gb", 128, 2048)
    s_bias = sb("s_bias", 128, 256)
    s_wtmp = sb("s_wtmp", 128, 2560)
    s_st = sb("s_st", 128, 164)
    s_gtm = sb("s_gtm", 128, 32)
    s_gw = sb("s_gw", 128, 64)
    s_cont = sb("s_cont", 64, 512)
    s_of = sb("s_of", 64, 512)
    s_eo = sb("s_eo", 64, 512)
    s_lg = sb("s_lg", 8, 512)
    s_s2c = sb("s_s2c", 128, 1024)
    s_rw1 = sb("s_rw1", 64, 64)
    s_rw2 = sb("s_rw2", 64, 8)
    s_encb = sb("s_encb", 128, 16)
    s_c2sb = sb("s_c2sb", 128, 16)
    s_decb = sb("s_decb", 128, 8)
    s_be1 = sb("s_be1", 128, 64)
    s_be2 = sb("s_be2", 64, 8)
    s_s2cb = sb("s_s2cb", 64, 1)
    s_rb1 = sb("s_rb1", 64, 1)
    s_rb2 = sb("s_rb2", 8, 1)
    s_gA = sb("s_gA", 128, 8)
    s_bA = sb("s_bA", 128, 8)
    s_ones = sb("s_ones", 1, 64)
    s_sel = sb("s_sel", 8, 512)
    s_we1h = sb("s_we1h", 128, 2048, BF16)
    s_we1l = sb("s_we1l", 128, 2048, BF16)
    s_we2h = sb("s_we2h", 128, 1024, BF16)
    s_we2l = sb("s_we2l", 128, 1024, BF16)
    s_hh = sb("s_hh", 128, 2048, BF16)
    s_hl = sb("s_hl", 128, 2048, BF16)
    s_conth = sb("s_conth", 128, 512, BF16)
    s_contl = sb("s_contl", 128, 512, BF16)

    pb = [nc.alloc_psum_tensor("pb%d" % i, [128, 512], F32).ap() for i in range(8)]

    def cons(slot):
        return s_cons[:, slot * 128:(slot + 1) * 128]
    FWD_HI = lambda s, kc: cons(s * 2 + kc)
    FWD_LO = lambda s, kc: cons(4 + s * 2 + kc)
    G_HI = lambda kc: cons(8 + kc)
    G_LO = lambda kc: cons(10 + kc)
    IDENT = cons(12)

    S = Sched()

    Bx = [Buf("x%d" % t) for t in range(NTQ)]
    Bz = [Buf("z%d" % t) for t in range(NTQ)]
    Bxn = [Buf("xn%d" % t) for t in range(NTQ)]
    Bat = [[Buf("at%d_%d" % (k, t)) for t in range(NTQ)] for k in range(8)]
    Bv = [Buf("v%d" % t) for t in range(NTQ)]
    Bbig = [Buf("big%d" % g) for g in range(16)]
    Bwb = [Buf("wb%d" % i) for i in range(2)]
    Bwe1 = [Buf("we1_%d" % i) for i in range(2)]
    Bwe2s = [Buf("we2s_%d" % i) for i in range(2)]
    Bbias = [Buf("bias%d" % i) for i in range(2)]
    Bp = [Buf("p%d" % i) for i in range(8)]
    Bcry_v = Buf("cry_v")
    Bcry_y = Buf("cry_y")
    Bst = [Buf("st%d" % i) for i in range(4)]
    Bgst = [Buf("gst%d" % i) for i in range(4)]
    Bgtm = Buf("gtm")
    Bgw = Buf("gw")
    Bcont = Buf("cont")
    Bcsp = Buf("csp")
    Bt1 = Buf("t1")
    Bof = Buf("of")
    Beo = Buf("eo")
    Bgbc = Buf("gbc")
    Blg = Buf("lg")
    Bwtmp = Buf("wtmp")
    Bwt = [Buf("wt0"), Buf("wt1")]
    Bhh = [Buf("hh%d" % i) for i in range(4)]
    Bhl = [Buf("hl%d" % i) for i in range(4)]
    Bksp = Buf("ksp")
    Bsetup = Buf("setup")
    Bgb = Buf("gb")

    def dma(engine, dst, src, reads=(), writes=(), deps=(), chan="gsetup", tag=""):
        slow = engine == "gpsimd"
        return S.add(engine,
                     lambda e, d_=dst, s_=src: e.dma_start(
                         d_, s_, allow_slow_non_contiguous=slow),
                     reads=reads, writes=writes, deps=deps, is_dma=True, chan=chan,
                     tag=tag)

    def xtile(buf_ap, t):
        return buf_ap[:, t * D:(t + 1) * D]

    def _memset_fn(v):
        v.memset(s_ones[:, :], 1.0)
        v.memset(s_st[:, 128:129], 1e-5)
        v.memset(s_cry[:, :], 0.0)
        v.memset(s_of[:, :], 0.0)
        v.memset(s_st[:, 129:157], 0.0)
        v.memset(s_st[:, 124:125], 1e-10)
        v.memset(s_st[:, 126:127], -1.0)
        return v.memset(s_st[:, 125:126], 1.0 + 1e-5)
    st_memset = S.add("vector", _memset_fn,
                      writes=[Bsetup, Bcry_v, Bcry_y, Bof], tag="memset")
    EPS = s_st[:, 128:129]
    EPS2 = s_st[:, 124:125]
    CM1 = s_st[:, 126:127]
    Z0 = s_st[:, 129:130]
    ONEPE = s_st[:, 125:126]

    dma("gpsimd", s_cons[:, 0:512],
        dcFhi.rearrange("s (kc p) b -> p s kc b", p=128), writes=[Bsetup], tag="cFhi")
    dma("gpsimd", s_cons[:, 512:1024],
        dcFlo.rearrange("s (kc p) b -> p s kc b", p=128), writes=[Bsetup], tag="cFlo")
    dma("gpsimd", s_cons[:, 1024:1280],
        dcGhi.rearrange("(kc p) b -> p kc b", p=128), writes=[Bsetup], tag="cGhi")
    dma("gpsimd", s_cons[:, 1280:1536],
        dcGlo.rearrange("(kc p) b -> p kc b", p=128), writes=[Bsetup], tag="cGlo")
    dma("gpsimd", s_cons[:, 1536:1664], dcI[:, :], writes=[Bsetup], tag="cI")
    dma("gpsimd", s_sel[:, :], dcSel[:, :], writes=[Bsetup], tag="cSel")
    dma("gpsimd", s_gb[:, 0:1024],
        dw["L0_ffn_ln_g"].unsqueeze(0).broadcast_to([128, 1024]), writes=[Bgb], tag="lng")
    dma("gpsimd", s_gb[:, 1024:2048],
        dw["L0_ffn_ln_b"].unsqueeze(0).broadcast_to([128, 1024]), writes=[Bgb], tag="lnb")
    dma("gpsimd", s_gA[:, :], dw["L0_attn_gamma"].rearrange("(kc p) -> p kc", p=128),
        writes=[Bsetup], tag="gA")
    dma("gpsimd", s_bA[:, :], dw["L0_attn_beta"].rearrange("(kc p) -> p kc", p=128),
        writes=[Bsetup], tag="bA")
    dma("gpsimd", s_s2c[:, :], dw["L0_ffn_s2c_W"].rearrange("(kc p) m -> p kc m", p=128),
        writes=[Bsetup], tag="s2cW")
    dma("gpsimd", s_rw1[:, :], dw["L0_ffn_rW1"][:, :], writes=[Bsetup], tag="rW1")
    dma("gpsimd", s_rw2[:, :], dw["L0_ffn_rW2"][:, :], writes=[Bsetup], tag="rW2")
    dma("gpsimd", s_encb[:, :], dw["L0_ffn_enc_b"].rearrange("(mc p) -> p mc", p=128),
        writes=[Bsetup], tag="encb")
    dma("gpsimd", s_c2sb[:, :], dw["L0_ffn_c2s_b"].rearrange("(mc p) -> p mc", p=128),
        writes=[Bsetup], tag="c2sb")
    dma("gpsimd", s_decb[:, :], dw["L0_ffn_dec_b"].rearrange("(mc p) -> p mc", p=128),
        writes=[Bsetup], tag="decb")
    dma("gpsimd", s_be1[:, :], dw["L0_ffn_be1"].rearrange("e (mc p) -> p e mc", p=128),
        writes=[Bsetup], tag="be1")
    dma("gpsimd", s_be2[:, :], dw["L0_ffn_be2"].rearrange("e p -> p e"),
        writes=[Bsetup], tag="be2")
    dma("gpsimd", s_s2cb[:, :], dw["L0_ffn_s2c_b"].unsqueeze(1), writes=[Bsetup], tag="s2cb")
    dma("gpsimd", s_rb1[:, :], dw["L0_ffn_rb1"].unsqueeze(1), writes=[Bsetup], tag="rb1")
    dma("gpsimd", s_rb2[:, :], dw["L0_ffn_rb2"].unsqueeze(1), writes=[Bsetup], tag="rb2")

    for cj, kname in enumerate(("L0_attn_k0", "L0_attn_k1")):
        dma("gpsimd", s_wtmp[:, 0:1024],
            dw[kname][0].rearrange("(c p) t -> p c t", p=128),
            writes=[Bwt[0]], deps=(), chan="ktap", tag="kload%d" % cj)
        for c4 in range(2):
            def tp_fn(t, c4=c4):
                last = None
                for ci in range(4):
                    c = c4 * 4 + ci
                    last = t.transpose(pb[7][:, ci * 128:(ci + 1) * 128],
                                       s_wtmp[:, c * 128:(c + 1) * 128], IDENT)
                return last
            stp = S.add("tensor", tp_fn, reads=[Bwt[0], Bsetup], writes=[Bp[7]],
                        tag="ktr%d_%d" % (cj, c4))
            S.add("vector", lambda v, c4=c4: v.tensor_copy(
                s_wtmp[:, 1024 + c4 * 512:1024 + (c4 + 1) * 512], pb[7][:, :]),
                reads=[Bp[7]], writes=[Bwt[1]], tag="ktr_ev%d_%d" % (cj, c4))
        for s_i in range(2):
            for half in range(2):
                def kf_fn(t, s_i=s_i, half=half):
                    rhs = s_wtmp[:, 1024 + half * 512:1024 + (half + 1) * 512]
                    t.matmul(pb[s_i][:, :], FWD_HI(s_i, 0), rhs, start=True, stop=False)
                    return t.matmul(pb[s_i][:, :], FWD_LO(s_i, 0), rhs, start=False, stop=True)
                S.add("tensor", kf_fn, reads=[Bwt[1], Bsetup], writes=[Bp[s_i]],
                      tag="kfwd%d_%d_%d" % (cj, s_i, half))
                S.add("vector", lambda v, s_i=s_i, half=half, cj=cj: v.tensor_copy(
                    s_ksp[:, cj * 2048 + s_i * 1024 + half * 512:][:, 0:512],
                    pb[s_i][:, :]),
                    reads=[Bp[s_i]], writes=[Bksp], tag="kev%d_%d_%d" % (cj, s_i, half))

    def KA(cj, s_i):
        return s_ksp[:, cj * 2048 + s_i * 1024: cj * 2048 + (s_i + 1) * 1024]

    Bpw1 = Buf("pw_we1")
    Bpw2 = Buf("pw_we2")

    def emit_presplit():
        ps_outs = []
        for e in range(8):
            slot = e % 2
            Bw1s = [Bbig[8 + 2 * slot], Bbig[9 + 2 * slot]]
            w32 = s_big[0:64, 4096 + slot * 1024:4096 + (slot + 1) * 1024]
            hst = s_we1h[0:64, 0:1024]
            lst = s_we1l[0:64, 0:1024]
            dma("gpsimd", w32, dw["L0_ffn_We1"][e, :, :], writes=Bw1s,
                chan="pi%d" % slot, tag="pw1i%d" % e)
            S.add("scalar", lambda sc, hst=hst, w32=w32: sc.copy(hst, w32),
                  reads=Bw1s, writes=[*Bw1s, Bwe1[0]], tag="pw1c%d" % e)
            S.add("gpsimd", lambda g, lst=lst, w32=w32, hst=hst: g.tensor_tensor(
                lst, w32, hst, Alu.subtract),
                reads=[*Bw1s, Bwe1[0]], writes=[*Bw1s, Bwe1[0]],
                tag="pw1s%d" % e)
            for half in range(2):
                ps_outs.append(dma(
                    "gpsimd", dwe1h[e, half * 64:(half + 1) * 64, :], hst,
                    reads=[Bwe1[0]],
                    chan="ph%d" % slot, tag="pw1oh%d_%d" % (e, half)))
                ps_outs.append(dma(
                    "gpsimd", dwe1l[e, half * 64:(half + 1) * 64, :], lst,
                    reads=[Bwe1[0]],
                    chan="pl%d" % slot, tag="pw1ol%d_%d" % (e, half)))
            Bw2s = [Bbig[12 + slot]]
            w232 = s_big[:, 6144 + slot * 512:6144 + (slot + 1) * 512]
            dma("gpsimd", w232.rearrange("p (hc m) -> p hc m", m=64),
                dw["L0_ffn_We2"][e, :, :].rearrange("(hc p) m -> p hc m", p=128),
                writes=Bw2s, chan="pi%d" % slot, tag="pw2i%d" % e)
            S.add("scalar", lambda sc, w232=w232: sc.copy(s_we2h[:, 0:512], w232),
                  reads=Bw2s, writes=[*Bw2s, Bwe2s[0]], tag="pw2c%d" % e)
            S.add("gpsimd", lambda g, w232=w232: g.tensor_tensor(
                s_we2l[:, 0:512], w232, s_we2h[:, 0:512], Alu.subtract),
                reads=[*Bw2s, Bwe2s[0]], writes=[*Bw2s, Bwe2s[0]], tag="pw2s%d" % e)
            ps_outs.append(dma(
                "gpsimd", dwe2h[e, :, :].rearrange("(hc p) m -> p hc m", p=128),
                s_we2h[:, 0:512].rearrange("p (hc m) -> p hc m", m=64),
                reads=[Bwe2s[0]], chan="qh%d" % slot, tag="pw2oh%d" % e))
            ps_outs.append(dma(
                "gpsimd", dwe2l[e, :, :].rearrange("(hc p) m -> p hc m", p=128),
                s_we2l[:, 0:512].rearrange("p (hc m) -> p hc m", m=64),
                reads=[Bwe2s[0]], chan="ql%d" % slot, tag="pw2ol%d" % e))
        S.add("gpsimd", lambda e_: e_.nop(), deps=ps_outs, writes=[Bpw1, Bpw2],
              tag="ps_fence")


    def ln_steps(src_ap, dst_ap, rb, wb_, tag, slot=0, extra_reads=()):
        o = slot * 16
        Bs = Bst[slot]
        st1 = S.add("vector", lambda v: (
            v.bn_stats(s_st[:, o:o + 6], src_ap[:, 0:512]),
            v.bn_stats(s_st[:, o + 6:o + 12], src_ap[:, 512:1024]))[-1],
            reads=[*rb, *extra_reads], writes=[Bs], tag=tag + "_stats")
        st2 = S.add("vector", lambda v: v.bn_aggr(s_st[:, o + 12:o + 14], s_st[:, o:o + 12]),
                    reads=[Bs], writes=[Bs], tag=tag + "_aggr")
        st3 = S.add("scalar", lambda sc: (
            sc.activation(s_st[:, o + 14:o + 15], s_st[:, o + 13:o + 14],
                          Act.Sqrt, bias=EPS),
            sc.drain(),
            sc.activation(s_st[:, o + 9:o + 10], s_st[:, o + 12:o + 13],
                          Act.Identity, bias=Z0, scale=CM1))[-1],
            reads=[Bs, Bsetup], writes=[Bs], tag=tag + "_sqrt")
        st4 = S.add("vector", lambda v: v.reciprocal(s_st[:, o + 15:o + 16], s_st[:, o + 14:o + 15]),
                    reads=[Bs], writes=[Bs], tag=tag + "_recip")
        st5 = S.add("scalar", lambda sc: (
            sc.activation(s_st[:, o + 8:o + 9], s_st[:, o + 15:o + 16],
                          Act.Identity, bias=Z0, scale=s_st[:, o + 9:o + 10]),
            sc.drain(),
            sc.activation(dst_ap, src_ap, Act.Identity,
                          bias=s_st[:, o + 8:o + 9], scale=s_st[:, o + 15:o + 16]))[-1],
            reads=[Bs, Bsetup, *rb], writes=[*wb_, Bs], tag=tag + "_apply")
        return st5

    Bst2 = [Buf("st2_%d" % i) for i in range(4)]

    O12 = [88, 104, 130, 146]

    def emit_ln12(qq, t, src, src_reads):
        o = O12[t]
        Bs = Bst2[t]
        S.add("vector", lambda v, o=o, src=src: (
            v.bn_stats(s_st[:, o:o + 6], src[:, 0:512]),
            v.bn_stats(s_st[:, o + 6:o + 12], src[:, 512:1024]))[-1],
            reads=list(src_reads), writes=[Bs], tag="q%d_ln12_%d_stats" % (qq, t))
        S.add("vector", lambda v, o=o: v.bn_aggr(
            s_st[:, o + 12:o + 14], s_st[:, o:o + 12]),
            reads=[Bs], writes=[Bs], tag="q%d_ln12_%d_aggr" % (qq, t))
        S.add("scalar", lambda sc, o=o: sc.activation(
            s_st[:, o + 8:o + 9], s_st[:, o + 13:o + 14], Act.Sqrt, bias=EPS),
            reads=[Bs, Bsetup], writes=[Bs], tag="q%d_ln12_%d_s1" % (qq, t))
        S.add("scalar", lambda sc, o=o: (
            sc.activation(s_st[:, o:o + 1], s_st[:, o + 13:o + 14],
                          Act.Identity, bias=EPS2, scale=EPS),
            sc.drain(),
            sc.activation(s_st[:, o + 9:o + 10], s_st[:, o + 13:o + 14],
                          Act.Sqrt, bias=s_st[:, o:o + 1]))[-1],
            reads=[Bs, Bsetup], writes=[Bs], tag="q%d_ln12_%d_s2" % (qq, t))
        S.add("scalar", lambda sc, o=o: sc.activation(
            s_st[:, o + 10:o + 11], s_st[:, o + 12:o + 13],
            Act.Identity, bias=Z0, scale=CM1),
            reads=[Bs, Bsetup], writes=[Bs], tag="q%d_ln12_%d_nm" % (qq, t))
        S.add("vector", lambda v, o=o: v.reciprocal(
            s_st[:, o + 14:o + 16], s_st[:, o + 8:o + 10]),
            reads=[Bs], writes=[Bs], tag="q%d_ln12_%d_rec" % (qq, t))
        S.add("scalar", lambda sc, o=o, src=src, t=t: (
            sc.activation(s_st[:, o + 11:o + 12], s_st[:, o + 14:o + 15],
                          Act.Identity, bias=Z0, scale=s_st[:, o + 10:o + 11]),
            sc.drain(),
            sc.activation(xtile(s_z, t), src, Act.Identity,
                          bias=s_st[:, o + 11:o + 12], scale=s_st[:, o + 14:o + 15]))[-1],
            reads=[Bs, Bsetup, *src_reads], writes=[Bz[t], Bs],
            tag="q%d_ln12_%d_zap" % (qq, t))
        S.add("scalar", lambda sc, o=o, src=src, t=t: (
            sc.activation(s_st[:, o + 11:o + 12], s_st[:, o + 15:o + 16],
                          Act.Identity, bias=Z0, scale=s_st[:, o + 10:o + 11]),
            sc.drain(),
            sc.activation(xtile(s_xn, t), src, Act.Identity,
                          bias=s_st[:, o + 11:o + 12], scale=s_st[:, o + 15:o + 16]))[-1],
            reads=[Bs, Bsetup, *src_reads], writes=[Bxn[t], Bs],
            tag="q%d_ln12_%d_xap" % (qq, t))

    def transpose_tile_to_at(tt, src_buf_ap, rbufs, tag, scale_col=None, bias_col=None):
        if True:
            for kc in range(8):
                bank = 4 + kc % 4
                S.add("tensor", lambda t, kc=kc, bank=bank, tt=tt: t.transpose(
                    pb[bank][:, tt * 128:(tt + 1) * 128],
                    src_buf_ap[:, tt * D + kc * 128: tt * D + (kc + 1) * 128],
                    IDENT),
                    reads=[*rbufs, Bsetup], writes=[Bp[bank]],
                    tag=tag + "_tp%d_%d" % (kc, tt))
                dst = s_at[:, kc * 512 + tt * 128: kc * 512 + (tt + 1) * 128]
                psl = pb[bank][:, tt * 128:(tt + 1) * 128]
                if scale_col is not None:
                    S.add("scalar", lambda sc, kc=kc, dst=dst, psl=psl: sc.activation(
                        dst, psl, Act.Identity,
                        bias=bias_col[:, kc:kc + 1], scale=scale_col[:, kc:kc + 1]),
                        reads=[Bp[bank], Bsetup], writes=[Bat[kc][tt]],
                        tag=tag + "_ev%d_%d" % (kc, tt))
                else:
                    S.add("scalar", lambda sc, dst=dst, psl=psl: sc.copy(dst, psl),
                          reads=[Bp[bank]], writes=[Bat[kc][tt]],
                          tag=tag + "_ev%d_%d" % (kc, tt))

    def transpose_to_at(src_buf_ap, rbufs_of_tile, tag, scale_col=None, bias_col=None):
        for tt in range(NTQ):
            transpose_tile_to_at(tt, src_buf_ap, rbufs_of_tile(tt), tag,
                                 scale_col=scale_col, bias_col=bias_col)

    def conv_steps(q, cj, src_ap, src_bufs, carry_buf, carry_ap, gate_ap_fn, gate_bufs,
                   dst_ap_fn, dst_bufs, tag):
        units = [(i, h) for i in range(NTQ) for h in range(2)]

        def emit_fwd(u):
            i, h = units[u]
            if i == 0:
                prev_ap, prev_b = carry_ap, [carry_buf]
            else:
                prev_ap = src_ap[:, (i - 1) * D: i * D]
                prev_b = src_bufs[i - 1]
            cur_ap = src_ap[:, i * D:(i + 1) * D]
            cur_b = src_bufs[i]
            par = u % 2
            b0, b1 = 2 * par, 2 * par + 1
            prevh = prev_ap[:, h * 512:(h + 1) * 512]
            curh = cur_ap[:, h * 512:(h + 1) * 512]

            def fwd_fn(t, b0=b0, b1=b1, prevh=prevh, curh=curh):
                last = None
                for s_i, bank in ((0, b0), (1, b1)):
                    last = t.matmul(pb[bank][:, :], FWD_HI(s_i, 0), prevh, start=True, stop=False)
                    last = t.matmul(pb[bank][:, :], FWD_HI(s_i, 1), curh, start=False, stop=False)
                    last = t.matmul(pb[bank][:, :], FWD_LO(s_i, 0), prevh, start=False, stop=False)
                    last = t.matmul(pb[bank][:, :], FWD_LO(s_i, 1), curh, start=False, stop=True)
                return last
            S.add("tensor", fwd_fn, reads=[*prev_b, *cur_b, Bsetup],
                  writes=[Bp[b0], Bp[b1]], tag="%s_fwd%d_%d" % (tag, i, h))

        def emit_rest(u):
            i, h = units[u]
            par = u % 2
            b0, b1 = 2 * par, 2 * par + 1
            binv = 4 + par
            A0 = KA(cj, 0)[:, h * 512:(h + 1) * 512]
            A1 = KA(cj, 1)[:, h * 512:(h + 1) * 512]
            w0 = s_wtmp[:, par * 1024:par * 1024 + 512]
            w1 = s_wtmp[:, par * 1024 + 512:par * 1024 + 1024]
            tmp = s_wtmp[:, 2048:2560]

            def pw_fn(v, b0=b0, b1=b1, A0=A0, A1=A1, w0=w0, w1=w1, tmp=tmp):
                v.tensor_tensor(tmp, pb[b1][:, :], A1, Alu.mult)
                v.tensor_tensor(w0, pb[b0][:, :], A0, Alu.mult)
                v.drain()
                v.tensor_tensor(w0, w0, tmp, Alu.subtract)
                v.drain()
                v.tensor_tensor(tmp, pb[b1][:, :], A0, Alu.mult)
                v.tensor_tensor(w1, pb[b0][:, :], A1, Alu.mult)
                v.drain()
                v.tensor_tensor(w1, w1, tmp, Alu.add)
                v.drain()
                v.tensor_tensor(w0[0:1, :], pb[b0][0:1, :], A0[0:1, :], Alu.mult)
                return v.tensor_tensor(w1[0:1, :], pb[b1][0:1, :], A1[0:1, :], Alu.mult)
            S.add("vector", pw_fn, reads=[Bp[b0], Bp[b1], Bksp],
                  writes=[Bwt[par], Bwtmp], tag="%s_pw%d_%d" % (tag, i, h))

            def inv_fn(t, binv=binv, w0=w0, w1=w1):
                t.matmul(pb[binv][:, :], G_HI(0), w0, start=True, stop=False)
                t.matmul(pb[binv][:, :], G_HI(1), w1, start=False, stop=False)
                t.matmul(pb[binv][:, :], G_LO(0), w0, start=False, stop=False)
                return t.matmul(pb[binv][:, :], G_LO(1), w1, start=False, stop=True)
            S.add("tensor", inv_fn, reads=[Bwt[par], Bsetup], writes=[Bp[binv]],
                  tag="%s_inv%d_%d" % (tag, i, h))
            gate_ap = gate_ap_fn(i, h)
            dst_ap = dst_ap_fn(i, h)
            S.add("vector", lambda v, binv=binv, gate_ap=gate_ap, dst_ap=dst_ap:
                  v.tensor_tensor(dst_ap, pb[binv][:, :], gate_ap, Alu.mult),
                  reads=[Bp[binv], gate_bufs(i, h)], writes=[dst_bufs(i, h)],
                  tag="%s_gate%d_%d" % (tag, i, h))

        emit_fwd(0)
        for u in range(1, len(units)):
            emit_fwd(u)
            emit_rest(u - 1)
        emit_rest(len(units) - 1)

    for q in range(NQ):
        r0 = q * TQ
        pipelined = (until != "attn")
        if q == 0 or not pipelined:
            for t in range(NTQ):
                dma("sync", xtile(s_x, t),
                    dx[r0 + t * 128:r0 + (t + 1) * 128, :],
                    writes=[Bx[t]], chan="x%d" % t, tag="q%d_xin%d" % (q, t))
            for t in range(NTQ):
                emit_ln12(q, t, xtile(s_x, t), [Bx[t]])
        else:
            for t in range(NTQ):
                S.add("gpsimd", lambda v, t=t: v.tensor_copy(
                    xtile(s_x, t), s_big[:, t * 1024:(t + 1) * 1024]),
                    reads=[Bbig[2 * t], Bbig[2 * t + 1]], writes=[Bx[t]],
                    tag="q%d_xcp%d" % (q, t))

        if q == 0 or not pipelined:
            transpose_to_at(s_xn, lambda tt: [Bxn[tt]], "q%d_xnT" % q,
                            scale_col=s_gA, bias_col=s_bA)
        if q == 0:
            emit_presplit()

        for ms in range(24):
            slot = ms % 2
            wap = s_wb[:, slot * 1024:(slot + 1) * 1024]
            dma("sync", wap.rearrange("p (kc m) -> p kc m", m=128),
                dw["L0_attn_W_in"][:, ms * 128:(ms + 1) * 128]
                .rearrange("(kc p) m -> p kc m", p=128),
                writes=[Bwb[slot]], chan="wb%d" % slot, tag="q%d_win%d" % (q, ms))
            dma("sync", s_bias[:, slot * 128:(slot + 1) * 128],
                dw["L0_attn_b_in"][ms * 128:(ms + 1) * 128].unsqueeze(0).broadcast_to([128, 128]),
                writes=[Bbias[slot]], chan="b%d" % slot, tag="q%d_binb%d" % (q, ms))
            bank = 4 + ms % 2

            for tt in range(NTQ):
                def mm_fn(t, bank=bank, wap=wap, tt=tt):
                    last = None
                    for kc in range(8):
                        last = t.matmul(
                            pb[bank][:, tt * 128:(tt + 1) * 128],
                            s_at[:, kc * 512 + tt * 128: kc * 512 + (tt + 1) * 128],
                            wap[:, kc * 128:(kc + 1) * 128],
                            start=(tt == 0 and kc == 0),
                            stop=(tt == NTQ - 1 and kc == 7))
                    return last
                S.add("tensor", mm_fn,
                      reads=[Bwb[slot]] + [Bat[kc][tt] for kc in range(8)],
                      writes=[Bp[bank]], tag="q%d_winmm%d_%d" % (q, ms, tt))
            fcol = (ms % 8) * 128
            if ms < 8:
                dst3 = s_v.rearrange("p (t d) -> p t d", d=D)[:, :, fcol:fcol + 128]
                dstbs = list(Bv)
            elif ms < 16:
                dst3 = s_big[:, 0:4096].rearrange("p (t d) -> p t d", d=D)[:, :, fcol:fcol + 128]
                dstbs = [Bbig[2 * tt + (1 if fcol >= 512 else 0)] for tt in range(NTQ)]
            else:
                dst3 = s_big[:, 4096:8192].rearrange("p (t d) -> p t d", d=D)[:, :, fcol:fcol + 128]
                dstbs = [Bbig[8 + 2 * tt + (1 if fcol >= 512 else 0)] for tt in range(NTQ)]
            S.add("vector", lambda v, dst3=dst3, bank=bank, slot=slot:
                  v.tensor_tensor(
                      dst3, pb[bank][:, :].rearrange("p (t c) -> p t c", c=128),
                      s_bias[:, slot * 128:(slot + 1) * 128]
                      .unsqueeze(1).broadcast_to([128, NTQ, 128]), Alu.add),
                  reads=[Bp[bank], Bbias[slot]], writes=dstbs,
                  tag="q%d_winev%d" % (q, ms))

        conv_steps(
            q, 0, s_v, [[Bv[i]] for i in range(4)], Bcry_v, s_cry[:, 0:1024],
            gate_ap_fn=lambda i, h: s_big[:, i * D + h * 512: i * D + (h + 1) * 512],
            gate_bufs=lambda i, h: Bbig[2 * i + h],
            dst_ap_fn=lambda i, h: s_big[:, i * D + h * 512: i * D + (h + 1) * 512],
            dst_bufs=lambda i, h: Bbig[2 * i + h],
            tag="q%d_c1" % q)
        S.add("vector", lambda v: v.tensor_copy(s_cry[:, 0:1024], s_v[:, 3 * D:4 * D]),
              reads=[Bv[3]], writes=[Bcry_v], tag="q%d_vcarry" % q)

        conv_steps(
            q, 1, s_big[:, 0:4096],
            [[Bbig[2 * i], Bbig[2 * i + 1]] for i in range(4)],
            Bcry_y, s_cry[:, 1024:2048],
            gate_ap_fn=lambda i, h: s_big[:, 4096 + i * D + h * 512: 4096 + i * D + (h + 1) * 512],
            gate_bufs=lambda i, h: Bbig[8 + 2 * i + h],
            dst_ap_fn=lambda i, h: s_big[:, 4096 + i * D + h * 512: 4096 + i * D + (h + 1) * 512],
            dst_bufs=lambda i, h: Bbig[8 + 2 * i + h],
            tag="q%d_c2" % q)
        S.add("vector", lambda v: v.tensor_copy(s_cry[:, 1024:2048], s_big[:, 3 * D:4 * D]),
              reads=[Bbig[6], Bbig[7]], writes=[Bcry_y], tag="q%d_ycarry" % q)

        transpose_to_at(s_big[:, 4096:8192], lambda tt: [Bbig[8 + 2 * tt], Bbig[8 + 2 * tt + 1]], "q%d_y2T" % q)
        for ms in range(8):
            slot = ms % 2
            wap = s_wb[:, slot * 1024:(slot + 1) * 1024]
            dma("sync", wap.rearrange("p (kc m) -> p kc m", m=128),
                dw["L0_attn_W_out"][:, ms * 128:(ms + 1) * 128]
                .rearrange("(kc p) m -> p kc m", p=128),
                writes=[Bwb[slot]], chan="wb%d" % slot, tag="q%d_wout%d" % (q, ms))
            dma("sync", s_bias[:, slot * 128:(slot + 1) * 128],
                dw["L0_attn_b_out"][ms * 128:(ms + 1) * 128].unsqueeze(0).broadcast_to([128, 128]),
                writes=[Bbias[slot]], chan="b%d" % slot, tag="q%d_boutb%d" % (q, ms))
            bank = 4 + ms % 2

            for tt in range(NTQ):
                def mm_fn(t, bank=bank, wap=wap, tt=tt):
                    last = None
                    for kc in range(8):
                        last = t.matmul(
                            pb[bank][:, tt * 128:(tt + 1) * 128],
                            s_at[:, kc * 512 + tt * 128: kc * 512 + (tt + 1) * 128],
                            wap[:, kc * 128:(kc + 1) * 128],
                            start=(tt == 0 and kc == 0),
                            stop=(tt == NTQ - 1 and kc == 7))
                    return last
                S.add("tensor", mm_fn,
                      reads=[Bwb[slot]] + [Bat[kc][tt] for kc in range(8)],
                      writes=[Bp[bank]], tag="q%d_woutmm%d_%d" % (q, ms, tt))
            fcol = ms * 128
            dst3 = s_xn.rearrange("p (t d) -> p t d", d=D)[:, :, fcol:fcol + 128]
            z3 = s_z.rearrange("p (t d) -> p t d", d=D)[:, :, fcol:fcol + 128]

            def wev_fn(v, dst3=dst3, z3=z3, bank=bank, slot=slot):
                v.tensor_tensor(
                    dst3, pb[bank][:, :].rearrange("p (t c) -> p t c", c=128),
                    s_bias[:, slot * 128:(slot + 1) * 128]
                    .unsqueeze(1).broadcast_to([128, NTQ, 128]), Alu.add)
                v.drain()
                return v.tensor_tensor(dst3, dst3, z3, Alu.add)
            S.add("vector", wev_fn, reads=[Bp[bank], Bbias[slot], *Bz],
                  writes=list(Bxn), tag="q%d_woutev%d" % (q, ms))

        for t in range(NTQ):
            ln_steps(xtile(s_xn, t), xtile(s_z, t), [Bxn[t]], [Bz[t]], "q%d_ln3_%d" % (q, t), slot=t)
            S.add("gpsimd", lambda v, t=t: v.tensor_tensor(
                xtile(s_x, t), xtile(s_x, t), xtile(s_z, t), Alu.add),
                reads=[Bx[t], Bz[t]], writes=[Bx[t]], tag="q%d_res1_%d" % (q, t))

        if until == "attn":
            for t in range(NTQ):
                dma("sync", dy[r0 + t * 128:r0 + (t + 1) * 128, :], xtile(s_x, t),
                    reads=[Bx[t]], chan="x%d" % t, tag="q%d_xout%d" % (q, t))
            continue
        for t in range(NTQ):
            ln_steps(xtile(s_x, t), xtile(s_z, t), [Bx[t]], [Bz[t]], "q%d_ln4_%d" % (q, t), slot=t)
        transpose_to_at(s_z, lambda tt: [Bz[tt]], "q%d_xfT" % q)

        of_init = S.add("vector", lambda v: v.memset(s_of[:, :], 0.0),
                        writes=[Bof], tag="q%d_ofinit" % q)
        for g in range(4):
            for kc in range(8):
                slot = kc % 2
                eap = s_wb[:, slot * 1024: slot * 1024 + 512]
                dma("sync", eap,
                    dw["L0_ffn_enc_W"][kc * 128:(kc + 1) * 128, g * 512:(g + 1) * 512],
                    writes=[Bwb[slot]], chan="wb%d" % slot, tag="q%d_enc%d_%d" % (q, g, kc))
                def mm_fn(t, g=g, kc=kc, eap=eap):
                    last = None
                    for mi in range(4):
                        last = t.matmul(pb[mi][:, :],
                                        eap[:, mi * 128:(mi + 1) * 128],
                                        s_at[:, kc * 512:(kc + 1) * 512],
                                        start=(kc == 0), stop=(kc == 7))
                    return last
                S.add("tensor", mm_fn, reads=[Bwb[slot]] + Bat[kc],
                      writes=[Bp[0], Bp[1], Bp[2], Bp[3]],
                      tag="q%d_encmm%d_%d" % (q, g, kc))
            for mi in range(4):
                mc = g * 4 + mi
                S.add("scalar", lambda sc, mc=mc, mi=mi: sc.activation(
                    s_big[:, mc * 512:(mc + 1) * 512], pb[mi][:, :], Act.Sigmoid,
                    bias=s_encb[:, mc:mc + 1]),
                    reads=[Bp[mi], Bsetup], writes=[Bbig[mc]],
                    tag="q%d_encev%d" % (q, mc))
                S.add("tensor", lambda t, mc=mc: t.matmul(
                    pb[6][0:64, :], s_s2c[:, mc * 64:(mc + 1) * 64],
                    s_big[:, mc * 512:(mc + 1) * 512],
                    start=(mc == 0), stop=(mc == 15)),
                    reads=[Bbig[mc], Bsetup], writes=[Bp[6]],
                    tag="q%d_s2c%d" % (q, mc))
        S.add("scalar", lambda sc: sc.activation(
            s_cont[:, :], pb[6][0:64, :], Act.Identity, bias=s_s2cb[:, 0:1]),
            reads=[Bp[6], Bsetup], writes=[Bcont], tag="q%d_contev" % q)

        def csp_fn(v):
            v.tensor_copy(s_conth[0:64, :], s_cont[:, :])
            v.drain()
            return v.tensor_tensor(s_contl[0:64, :], s_cont[:, :], s_conth[0:64, :],
                                   Alu.subtract)
        S.add("vector", csp_fn, reads=[Bcont], writes=[Bcsp], tag="q%d_contsp" % q)
        dma("gpsimd", s_conth[64:128, :], s_conth[0:64, :], reads=[Bcsp],
            writes=[Bcsp], chan="dup", tag="q%d_cdup_h" % q)
        dma("gpsimd", s_contl[64:128, :], s_contl[0:64, :], reads=[Bcsp],
            writes=[Bcsp], chan="dup", tag="q%d_cdup_l" % q)

        S.add("tensor", lambda t: t.matmul(pb[7][0:64, :], s_rw1[:, :], s_cont[:, :],
                                           start=True, stop=True),
              reads=[Bcont, Bsetup], writes=[Bp[7]], tag="q%d_rw1" % q)
        S.add("scalar", lambda sc: sc.activation(s_eo[:, :], pb[7][0:64, :], Act.Tanh,
                                                 bias=s_rb1[:, 0:1]),
              reads=[Bp[7], Bsetup], writes=[Beo], tag="q%d_tanh" % q)
        S.add("tensor", lambda t: t.matmul(pb[6][0:8, :], s_rw2[:, :], s_eo[:, :],
                                           start=True, stop=True),
              reads=[Beo, Bsetup], writes=[Bp[6]], tag="q%d_rw2" % q)
        S.add("scalar", lambda sc: sc.activation(s_lg[0:8, 0:512], pb[6][0:8, :],
                                                 Act.Identity, bias=s_rb2[:, 0:1]),
              reads=[Bp[6], Bsetup], writes=[Blg], tag="q%d_logev" % q)
        def expert_mm(e):
            eslot = e % 2
            w1h = s_we1h[:, eslot * 1024:(eslot + 1) * 1024]
            w1l = s_we1l[:, eslot * 1024:(eslot + 1) * 1024]
            w2h = s_we2h[:, eslot * 512:(eslot + 1) * 512]
            w2l = s_we2l[:, eslot * 512:(eslot + 1) * 512]
            dma("sync", w1h, dwe1h[e, :, :], reads=[Bpw1], writes=[Bwe1[eslot]],
                chan="we1_%d" % eslot, tag="q%d_we1h_%d" % (q, e))
            dma("gpsimd", w1l, dwe1l[e, :, :], reads=[Bpw1], writes=[Bwe1[eslot]],
                chan="wl1_%d" % eslot, tag="q%d_we1l_%d" % (q, e))
            dma("sync", w2h.rearrange("p (hc m) -> p hc m", m=64),
                dwe2h[e, :, :].rearrange("(hc p) m -> p hc m", p=128),
                reads=[Bpw2], writes=[Bwe2s[eslot]], chan="we2h%d" % eslot,
                tag="q%d_we2h_%d" % (q, e))
            dma("gpsimd", w2l.rearrange("p (hc m) -> p hc m", m=64),
                dwe2l[e, :, :].rearrange("(hc p) m -> p hc m", p=128),
                reads=[Bpw2], writes=[Bwe2s[eslot]], chan="we2l%d" % eslot,
                tag="q%d_we2l_%d" % (q, e))

            def we1_pair(j):
                for half in range(2):
                    mc = 2 * j + half
                    hb = 2 * (j % 2) + half
                    bank = (j % 2) * 2 + half
                    p0, p1 = 64 * half, 64 * (half + 1)
                    S.add("tensor", lambda t, mc=mc, bank=bank, p0=p0, p1=p1,
                          w1h=w1h, w1l=w1l: (
                        t.matmul(pb[bank][:, :],
                                 w1h[p0:p1, mc * 128:(mc + 1) * 128],
                                 s_conth[p0:p1, :], start=True, stop=False,
                                 tile_position=(p0, 0)),
                        t.matmul(pb[bank][:, :],
                                 w1h[p0:p1, mc * 128:(mc + 1) * 128],
                                 s_contl[p0:p1, :], start=False, stop=False,
                                 tile_position=(p0, 0)),
                        t.matmul(pb[bank][:, :],
                                 w1l[p0:p1, mc * 128:(mc + 1) * 128],
                                 s_conth[p0:p1, :], start=False, stop=True,
                                 tile_position=(p0, 0)))[-1],
                        reads=[Bwe1[eslot], Bcsp], writes=[Bp[bank]],
                        tag="q%d_we1mm%d_%d" % (q, e, mc))
                    t32 = s_wtmp[:, half * 1024: half * 1024 + 512]

                    def hrl_fn(sc, mc=mc, hb=hb, e=e, t32=t32, bank=bank):
                        sc.activation(t32, pb[bank][:, :], Act.Relu,
                                      bias=s_be1[:, e * 8 + mc:e * 8 + mc + 1])
                        sc.drain()
                        return sc.copy(s_hh[:, hb * 512:(hb + 1) * 512], t32)
                    S.add("scalar", hrl_fn, reads=[Bp[bank], Bsetup],
                          writes=[Bhh[hb], Bwt[half]], tag="q%d_hrl%d_%d" % (q, e, mc))
                    S.add("vector", lambda v, hb=hb, t32=t32: v.tensor_tensor(
                        s_hl[:, hb * 512:(hb + 1) * 512], t32,
                        s_hh[:, hb * 512:(hb + 1) * 512], Alu.subtract),
                        reads=[Bhh[hb], Bwt[half]], writes=[Bhl[hb]],
                        tag="q%d_hsb%d_%d" % (q, e, mc))

            def we2_mm(mc):
                hb = 2 * ((mc // 2) % 2) + mc % 2
                bk = 6 if e % 2 == 0 else 4
                S.add("tensor", lambda t, mc=mc, hb=hb, bk=bk: (
                    t.matmul(pb[bk][0:64, :], w2h[:, mc * 64:(mc + 1) * 64],
                             s_hh[:, hb * 512:(hb + 1) * 512],
                             start=(mc == 0), stop=False),
                    t.matmul(pb[bk][0:64, :], w2l[:, mc * 64:(mc + 1) * 64],
                             s_hh[:, hb * 512:(hb + 1) * 512],
                             start=False, stop=False))[-1],
                    reads=[Bhh[hb], Bwe2s[eslot]], writes=[Bp[bk]],
                    tag="q%d_we2mma%d_%d" % (q, e, mc))
                S.add("tensor", lambda t, mc=mc, hb=hb, bk=bk: t.matmul(
                    pb[bk][0:64, :], w2h[:, mc * 64:(mc + 1) * 64],
                    s_hl[:, hb * 512:(hb + 1) * 512],
                    start=False, stop=(mc == 7)),
                    reads=[Bhl[hb], Bwe2s[eslot]], writes=[Bp[bk]],
                    tag="q%d_we2mmb%d_%d" % (q, e, mc))

            we1_pair(0)
            for j in range(1, 4):
                we1_pair(j)
                we2_mm(2 * j - 2)
                we2_mm(2 * j - 1)
            we2_mm(6)
            we2_mm(7)

        def expert_gate(e):
                S.add("tensor", lambda t, e=e: t.matmul(
                    pb[7][0:64, :], s_sel[:, e * 64:(e + 1) * 64], s_lg[0:8, :],
                    start=True, stop=True),
                    reads=[Blg, Bsetup], writes=[Bp[7]], tag="q%d_gbc%d" % (q, e))

                bk = 6 if e % 2 == 0 else 4
                S.add("vector", lambda v, e=e, bk=bk: v.tensor_scalar(
                    s_eo[:, :], pb[bk][0:64, :], s_be2[:, e:e + 1], None, Alu.add),
                    reads=[Bp[bk], Bsetup], writes=[Beo], tag="q%d_eoev%d" % (q, e))
                S.add("vector", lambda v: v.tensor_tensor(s_eo[:, :], s_eo[:, :],
                                                          pb[7][0:64, :], Alu.mult),
                      reads=[Beo, Bp[7]], writes=[Beo], tag="q%d_eog%d" % (q, e))
                S.add("vector", lambda v: v.tensor_tensor(s_of[:, :], s_of[:, :], s_eo[:, :],
                                                          Alu.add),
                      reads=[Bof, Beo], writes=[Bof], tag="q%d_ofacc%d" % (q, e))

        expert_mm(0)


        def lgt_fn(t):
            last = None
            for tt in range(NTQ):
                last = t.transpose(pb[7][:, tt * 8:(tt + 1) * 8],
                                   s_lg[0:8, tt * 128:(tt + 1) * 128], IDENT[0:8, 0:8])
            return last
        S.add("tensor", lgt_fn, reads=[Blg, Bsetup], writes=[Bp[7]], tag="q%d_lgT" % q)
        S.add("vector", lambda v: v.tensor_copy(s_gtm[:, :], pb[7][:, 0:32]),
              reads=[Bp[7]], writes=[Bgtm], tag="q%d_lgTev" % q)
        lg3 = s_gtm[:, 0:32].rearrange("p (t e) -> p t e", e=8)
        pr3 = s_gw[:, 0:32].rearrange("p (t e) -> p t e", e=8)
        mk3 = s_gw[:, 32:64].rearrange("p (t e) -> p t e", e=8)
        NEG4 = s_st[:, 64:68]
        SUM4 = s_st[:, 68:72]
        M1_4 = s_st[:, 72:76]
        M2_4 = s_st[:, 76:80]
        S12_4 = s_st[:, 80:84]
        REC4 = s_st[:, 84:88]
        ZCOL = s_st[:, 129:130]
        bc = lambda a: a.unsqueeze(2).broadcast_to([128, 4, 8])
        S.add("vector", lambda v: v.tensor_reduce(NEG4, lg3, AX, Alu.max, negate=True),
              reads=[Bgtm], writes=[Bgst[0]], tag="q%d_gm" % q)
        S.add("vector", lambda v: v.tensor_tensor(lg3, lg3, bc(NEG4), Alu.add),
              reads=[Bgtm, Bgst[0]], writes=[Bgtm], tag="q%d_gsub" % q)
        S.add("scalar", lambda sc: sc.activation(s_gw[:, 0:32], s_gtm[:, 0:32],
                                                 Act.Exp, bias=ZCOL),
              reads=[Bgtm, Bsetup], writes=[Bgw], tag="q%d_gexp" % q)
        S.add("vector", lambda v: v.tensor_reduce(SUM4, pr3, AX, Alu.add),
              reads=[Bgw], writes=[Bgst[0]], tag="q%d_gsum" % q)
        S.add("vector", lambda v: v.reciprocal(REC4, SUM4),
              reads=[Bgst[0]], writes=[Bgst[0]], tag="q%d_grec" % q)
        S.add("vector", lambda v: v.tensor_tensor(pr3, pr3, bc(REC4), Alu.mult),
              reads=[Bgw, Bgst[0]], writes=[Bgw], tag="q%d_gnorm" % q)
        S.add("vector", lambda v: v.tensor_reduce(M1_4, pr3, AX, Alu.max),
              reads=[Bgw], writes=[Bgst[0]], tag="q%d_gm1" % q)
        S.add("vector", lambda v: v.tensor_tensor(mk3, pr3, bc(M1_4), Alu.is_lt),
              reads=[Bgw, Bgst[0]], writes=[Bgw], tag="q%d_gmask" % q)
        S.add("vector", lambda v: v.tensor_tensor(mk3, mk3, pr3, Alu.mult),
              reads=[Bgw], writes=[Bgw], tag="q%d_gmask2" % q)
        S.add("vector", lambda v: v.tensor_reduce(M2_4, mk3, AX, Alu.max),
              reads=[Bgw], writes=[Bgst[0]], tag="q%d_gm2" % q)
        S.add("vector", lambda v: v.tensor_tensor(S12_4, M1_4, M2_4, Alu.add),
              reads=[Bgst[0]], writes=[Bgst[0]], tag="q%d_gs12" % q)
        S.add("vector", lambda v: v.reciprocal(REC4, S12_4),
              reads=[Bgst[0]], writes=[Bgst[0]], tag="q%d_grec2" % q)
        S.add("vector", lambda v: v.tensor_tensor(mk3, pr3, bc(M2_4), Alu.is_ge),
              reads=[Bgw, Bgst[0]], writes=[Bgw], tag="q%d_gkeep" % q)
        S.add("vector", lambda v: v.tensor_tensor(pr3, pr3, bc(REC4), Alu.mult),
              reads=[Bgw, Bgst[0]], writes=[Bgw], tag="q%d_gsc" % q)
        S.add("vector", lambda v: v.tensor_tensor(lg3, pr3, mk3, Alu.mult),
              reads=[Bgw], writes=[Bgtm], tag="q%d_gfin" % q)
        def gT_fn(t):
            last = None
            for tt in range(NTQ):
                last = t.transpose(pb[7][0:8, tt * 128:(tt + 1) * 128],
                                   s_gtm[:, tt * 8:(tt + 1) * 8], IDENT)
            return last
        expert_mm(1)
        S.add("tensor", gT_fn, reads=[Bgtm, Bsetup], writes=[Bp[7]], tag="q%d_gT" % q)
        S.add("vector", lambda v: v.tensor_copy(s_lg[0:8, :], pb[7][0:8, :]),
              reads=[Bp[7]], writes=[Blg], tag="q%d_gTev" % q)
        expert_gate(0)
        for e in range(2, 8):
            expert_mm(e)
            expert_gate(e - 1)
        expert_gate(7)

        for half in range(2):
            dma("sync", s_wb[0:64, half * 1024:(half + 1) * 1024],
                dw["L0_ffn_c2s_W"][:, half * 1024:(half + 1) * 1024],
                writes=[Bwb[half]], chan="wb%d" % half, tag="q%d_c2sw%d" % (q, half))
        for mc in range(16):
            bank = 2 + mc % 2
            half = mc // 8
            S.add("tensor", lambda t, mc=mc, bank=bank, half=half: t.matmul(
                pb[bank][:, :],
                s_wb[0:64, half * 1024 + (mc % 8) * 128: half * 1024 + (mc % 8 + 1) * 128],
                s_of[:, :], start=True, stop=True),
                reads=[Bof, Bwb[half]], writes=[Bp[bank]], tag="q%d_c2s%d" % (q, mc))
            S.add("scalar", lambda sc, mc=mc, bank=bank: sc.activation(
                s_big[:, mc * 512:(mc + 1) * 512], pb[bank][:, :], Act.Sigmoid,
                bias=s_c2sb[:, mc:mc + 1]),
                reads=[Bp[bank], Bsetup], writes=[Bbig[mc]], tag="q%d_c2sev%d" % (q, mc))

        for g2 in range(2):
            for kc in range(16):
                slot = kc % 2
                dap = s_wb[:, slot * 1024: slot * 1024 + 512]
                dma("sync", dap,
                    dw["L0_ffn_dec_W"][kc * 128:(kc + 1) * 128, g2 * 512:(g2 + 1) * 512],
                    writes=[Bwb[slot]], chan="wb%d" % slot, tag="q%d_dec%d_%d" % (q, g2, kc))
                def mm_fn(t, g2=g2, kc=kc, dap=dap):
                    last = None
                    for mi in range(4):
                        last = t.matmul(pb[mi][:, :],
                                        dap[:, mi * 128:(mi + 1) * 128],
                                        s_big[:, kc * 512:(kc + 1) * 512],
                                        start=(kc == 0), stop=(kc == 15))
                    return last
                S.add("tensor", mm_fn, reads=[Bwb[slot], Bbig[kc]],
                      writes=[Bp[0], Bp[1], Bp[2], Bp[3]],
                      tag="q%d_decmm%d_%d" % (q, g2, kc))
            for mi in range(4):
                m = g2 * 4 + mi
                S.add("scalar", lambda sc, m=m, mi=mi: sc.activation(
                    s_xn[:, m * 512:(m + 1) * 512], pb[mi][:, :], Act.Sigmoid,
                    bias=s_decb[:, m:m + 1]),
                    reads=[Bp[mi], Bsetup], writes=[Bxn[m // 2]],
                    tag="q%d_decev%d" % (q, m))

        if pipelined and q + 1 < NQ:
            r1_ = (q + 1) * TQ
            for t in range(NTQ):
                dma("sync", s_big[:, t * 1024:(t + 1) * 1024],
                    dx[r1_ + t * 128:r1_ + (t + 1) * 128, :],
                    writes=[Bbig[2 * t], Bbig[2 * t + 1]], chan="x%d" % t,
                    tag="q%d_xin%d" % (q + 1, t))

        for m in range(8):
            bank = 6 + m % 2

            def tp_fn(t, m=m, bank=bank):
                last = None
                for tt in range(NTQ):
                    last = t.transpose(pb[bank][:, tt * 128:(tt + 1) * 128],
                                       s_xn[:, m * 512 + tt * 128: m * 512 + (tt + 1) * 128],
                                       IDENT)
                return last
            S.add("tensor", tp_fn, reads=[*Bxn, Bsetup], writes=[Bp[bank]],
                  tag="q%d_decT%d" % (q, m))

            if m % 2 == 0:
                def ev_fn(v, m=m, bank=bank):
                    last = None
                    for tt in range(NTQ):
                        last = v.tensor_copy(
                            s_z[:, tt * D + m * 128: tt * D + (m + 1) * 128],
                            pb[bank][:, tt * 128:(tt + 1) * 128])
                    return last
            else:
                def ev_fn(sc, m=m, bank=bank):
                    last = None
                    for tt in range(NTQ):
                        if tt:
                            sc.drain()
                        last = sc.copy(
                            s_z[:, tt * D + m * 128: tt * D + (m + 1) * 128],
                            pb[bank][:, tt * 128:(tt + 1) * 128])
                    return last
            S.add("vector" if m % 2 == 0 else "scalar", ev_fn,
                  reads=[Bp[bank]], writes=list(Bz),
                  tag="q%d_decTev%d" % (q, m))
        for t in range(NTQ):
            ln_steps(xtile(s_z, t), xtile(s_v, t), [Bz[t]], [Bv[t]],
                     "q%d_ln5_%d" % (q, t), slot=t)
            if pipelined and q + 1 < NQ:
                emit_ln12(q + 1, t, s_big[:, t * 1024:(t + 1) * 1024],
                          [Bbig[2 * t], Bbig[2 * t + 1]])
                transpose_tile_to_at(t, s_xn, [Bxn[t]], "q%d_xnT" % (q + 1),
                                     scale_col=s_gA, bias_col=s_bA)
            S.add("gpsimd", lambda v, t=t: v.tensor_tensor(
                xtile(s_v, t), xtile(s_v, t), s_gb[:, 0:1024], Alu.mult),
                reads=[Bv[t], Bgb], writes=[Bv[t]], tag="q%d_lng_%d" % (q, t))
            S.add("gpsimd", lambda v, t=t: v.tensor_tensor(
                xtile(s_v, t), xtile(s_v, t), s_gb[:, 1024:2048], Alu.add),
                reads=[Bv[t], Bgb], writes=[Bv[t]], tag="q%d_lnb_%d" % (q, t))
            ln_steps(xtile(s_v, t), xtile(s_v, t), [Bv[t]], [Bv[t]],
                     "q%d_ln6_%d" % (q, t), slot=t)
            S.add("gpsimd", lambda v, t=t: v.tensor_tensor(
                xtile(s_x, t), xtile(s_x, t), xtile(s_v, t), Alu.add),
                reads=[Bx[t], Bv[t]], writes=[Bx[t]],
                tag="q%d_res2_%d" % (q, t))

        for t in range(NTQ):
            dma("scalar", dy[r0 + t * 128:r0 + (t + 1) * 128, :], xtile(s_x, t),
                reads=[Bx[t]], chan="xo%d" % t, tag="q%d_xout%d" % (q, t))

    out_steps = [st for st in S.steps if "_xout" in st.tag]
    S.add("sync", lambda e: e.nop(), deps=out_steps, tag="fence")
    return nc, S


def _dma_instr_counts(nc, S):
    per_engine_steps = {}
    for st in S.steps:
        if st.is_dma:
            per_engine_steps.setdefault(st.engine, []).append(st)
    eng_map = {"sync": "SP", "gpsimd": "Pool", "scalar": "Activation",
               "vector": "DVE", "tensor": "PE"}
    counts = {}
    idx_by_eng = {}
    for blkm in nc.m.functions[0].blocks:
        insts = list(blkm.instructions)
        if not insts:
            continue
        name = str(insts[0].engine).split(".")[-1]
        steps = eng = None
        for k, v in eng_map.items():
            if v == name:
                steps, eng = per_engine_steps.get(k), k
        if not steps:
            continue
        idx = idx_by_eng.get(eng, 0)
        run = 0
        for i in insts:
            nm = type(i).__name__
            if nm == "InstDMACopy":
                run += 1
            elif nm == "InstNoOp" and run > 0:
                counts[steps[idx].i] = run
                idx += 1
                run = 0
        idx_by_eng[eng] = idx
    for eng, steps in per_engine_steps.items():
        assert idx_by_eng.get(eng, 0) == len(steps), \
            "dma walk mismatch %s: %d vs %d" % (eng, idx_by_eng.get(eng, 0), len(steps))
    return counts


def build_nc(T, until="full"):
    nc0, S0 = _build(T, until)
    with nc0.Block() as blk0:
        S0.emit(nc0, blk0, count_mode=True)
    counts = _dma_instr_counts(nc0, S0)
    nc, S = _build(T, until)
    weights = {}
    for st0, st in zip(S0.steps, S.steps):
        if st.is_dma and st0.i in counts:
            weights[st.i] = counts[st0.i]
    with nc.Block() as blk:
        S.emit(nc, blk, dma_weights=weights)
    return nc


_CACHE = {}


def kernel(**inputs):
    x = np.ascontiguousarray(np.asarray(inputs["L0_attn_x"], dtype=np.float32))
    B, T, D_ = x.shape
    assert D_ == D and B == 8
    nc = build_nc(T)
    consts = make_consts()
    base = {k: np.ascontiguousarray(np.asarray(inputs[k], dtype=np.float32))
            for k in WEIGHTS}
    base.update(consts)
    in_maps = [dict(base, x=x[i]) for i in range(B)]
    res = run_bass_kernel_spmd(nc, in_maps, core_ids=list(range(B)))
    return np.stack([np.asarray(res.results[i]["y"], dtype=np.float32)
                     for i in range(B)], axis=0)
